# version 06
# baseline (speedup 1.0000x reference)
import contextlib
import numpy as np
import ml_dtypes
import concourse.bass as bass
import concourse.mybir as mybir
from concourse.bass import broadcast_tensor_aps
from concourse.bass_utils import run_bass_kernel_spmd

F32 = mybir.dt.float32
BF16 = mybir.dt.bfloat16
AF = mybir.ActivationFunctionType
ALU = mybir.AluOpType
AX = mybir.AxisListType
BF16NP = ml_dtypes.bfloat16

T = 2048
D = 1024
NCT = 8
NTT = 16
H = 16
DH = 64
TCH = 512
NCH = T // TCH

CC_VALS = [1e-5, -0.5, 1.0, 1e-8, 1e-6, 1e-12, 0.1, 10.0, -1.0, 0.5, 8.0, -0.125, 0.125]
CCI = {v: i for i, v in enumerate(CC_VALS)}


class Prog:
    def __init__(self):
        self.nc = bass.Bass("TRN2", target_bir_lowering=False)
        self.es = contextlib.ExitStack()
        self._sems = {}
        self.cnt = {}
        self.host = {}
        self.host_percore = {}
        self.out_names = []
        self.C = {}
        self.uid = 0

    def sem(self, name):
        if name not in self._sems:
            self._sems[name] = self.es.enter_context(self.nc.semaphore(name))
            self.cnt[name] = 0
        return self._sems[name]

    def inc(self, inst, name, k=1):
        inst.then_inc(self.sem(name), k)
        self.cnt[name] += k
        return self.cnt[name]

    def mark(self, eng, name):
        return self.inc(eng.nop(), name, 1)

    def wait(self, eng, name, v=None):
        eng.wait_ge(self.sem(name), self.cnt[name] if v is None else v)

    def esw(self, inst, name):
        eng = {"v": self.nc.vector, "s": self.nc.scalar, "g": self.nc.gpsimd}[name]
        vv = self.inc(inst, name)
        eng.wait_ge(self.sem(name), vv)
        return vv

    def din(self, name, arr, dtype=None, percore=False):
        a0 = arr[0] if percore else arr
        dt = dtype or (F32 if a0.dtype == np.float32 else BF16)
        h = self.nc.dram_tensor(name, list(a0.shape), dt, kind="ExternalInput").ap()
        if percore:
            self.host_percore[name] = arr
        else:
            self.host[name] = arr
        return h

    def dout(self, name, shape, dtype=F32):
        self.out_names.append(name)
        return self.nc.dram_tensor(name, list(shape), dtype, kind="ExternalOutput").ap()

    def dram(self, name, shape, dtype=F32):
        return self.nc.dram_tensor(name, list(shape), dtype, kind="Internal").ap()

    def sb(self, name, shape, dtype=F32, scope=None):
        cm = self.nc.sbuf_tensor(name, list(shape), dtype)
        return (scope or self.es).enter_context(cm)

    def ps(self, name, shape, dtype=F32, scope=None):
        cm = self.nc.psum_tensor(name, list(shape), dtype)
        return (scope or self.es).enter_context(cm)

    def engines(self):
        nc = self.nc
        return nc.tensor, nc.vector, nc.scalar, nc.gpsimd, nc.sync

    def barrier(self, tag="bar"):
        t, v, s, g, sy = self.engines()
        for e, nm in ((t, "t"), (v, "v"), (s, "s"), (g, "g")):
            if self.cnt.get(nm, 0) > 0:
                e.wait_ge(self.sem(nm), self.cnt[nm])
        engs = [t, v, s, g, sy]
        sm = self.sem(tag)
        for e in engs:
            self.inc(e.nop(), tag, 1)
        tot = self.cnt[tag]
        for e in engs:
            e.wait_ge(sm, tot)

    def cc(self, val, parts=128):
        return self.C["cc"][0:parts, CCI[val]:CCI[val] + 1]



def host_consts():
    C = {}
    C["cc"] = np.tile(np.array([CC_VALS], np.float32), (128, 1))
    C["ones_col_b"] = np.ones((128, 1), BF16NP)
    C["ones_row_b"] = np.ones((1, 128), BF16NP)
    C["ones_row_f"] = np.ones((1, 128), np.float32)
    C["ones_col_f"] = np.ones((128, 1), np.float32)
    C["identity"] = np.eye(128, dtype=np.float32)
    hp = np.zeros((8, 128, 16), np.float32)
    for ct in range(8):
        hp[ct, 0:64, 2 * ct] = 1
        hp[ct, 64:128, 2 * ct + 1] = 1
    C["hpsel"] = np.ascontiguousarray(hp.transpose(1, 0, 2)).astype(BF16NP)
    bc = np.zeros((8, 16, 128), np.float32)
    for ct in range(8):
        bc[ct, 2 * ct, 0:64] = 1
        bc[ct, 2 * ct + 1, 64:128] = 1
    C["bc16"] = np.ascontiguousarray(bc.transpose(1, 0, 2)).astype(BF16NP)
    sig = np.zeros((128, 128), np.float32)
    for m in range(128):
        sm = (m // 64) * 64 + ((m % 64 + 32) % 64)
        sig[sm, m] = 1
    C["psig"] = sig.astype(BF16NP)
    ql, kl = np.meshgrid(np.arange(128), np.arange(128), indexing="ij")
    C["tri_q0"] = (kl < ql).astype(np.float32).astype(BF16NP)
    C["tri_q2"] = (kl > ql).astype(np.float32).astype(BF16NP)
    C["tri_t0"] = np.ascontiguousarray(C["tri_q0"].T)
    C["tri_t2"] = np.ascontiguousarray(C["tri_q2"].T)
    C["negi"] = (np.eye(128) * -1e9).astype(BF16NP)
    ns = np.zeros((16, 16, 128), np.float32)
    for hh in range(16):
        ns[hh, hh, :] = -1.0
    C["negsel"] = ns
    return C


def load_consts(p, names=None):
    hc = host_consts()
    for name, arr in hc.items():
        if names is not None and name not in names:
            continue
        h = p.din("c_" + name, arr)
        sbt = p.sb("C" + name, arr.shape, F32 if arr.dtype == np.float32 else BF16)
        p.inc(p.nc.sync.dma_start(sbt[...], h), "qs", 16)
        p.C[name] = sbt



class Banks:
    def __init__(self, p, n=8, name="pb"):
        self.p = p
        self.t = [p.ps(f"{name}{i}", [128, 512]) for i in range(n)]
        self.rd = [[] for _ in range(n)]
        self.wr = [None] * n

    def mm_wait(self, i):
        for tok in self.rd[i]:
            self.p.nc.tensor.wait_ge(self.p.sem(tok[0]), tok[1])
        self.rd[i] = []

    def wrote(self, i, tok):
        self.wr[i] = tok

    def rd_wait(self, eng, i):
        if self.wr[i] is not None:
            eng.wait_ge(self.p.sem(self.wr[i][0]), self.wr[i][1])

    def read(self, i, tok):
        self.rd[i].append(tok)



def emit_ln(p, bk, src, xn_out, xn_f32=None, scale_cols=None, bias_cols=None,
            src_tok=None, eps=1e-5):
    t, v, s, g, sy = p.engines()
    p.barrier("phb")
    u = p.uid; p.uid += 1
    sc = contextlib.ExitStack()
    tmpb = p.sb(f"lnb{u}", [128, NCT, TCH], BF16, sc)
    tmpq = p.sb(f"lnq{u}", [128, NCT, TCH], BF16, sc)
    rows = p.sb(f"lnr{u}", [1, 4, TCH], F32, sc)
    rows_b = p.sb(f"lnrb{u}", [1, 2, TCH], BF16, sc)
    ps_s, ps_q = bk.t[0][0:1, :], bk.t[1][0:1, :]
    ps_mu, ps_rs = bk.t[2], bk.t[3]
    ones_b = p.C["ones_col_b"]
    onesr = p.C["ones_row_b"]
    eps_c = p.cc(eps, 1)
    for c in range(NCH):
        n0 = c * TCH
        if src_tok is not None:
            v.wait_ge(p.sem(src_tok[0]), src_tok[1])
            s.wait_ge(p.sem(src_tok[0]), src_tok[1])
        if c > 0:
            v.wait_ge(p.sem("v"), p.cnt["v"])
            v.wait_ge(p.sem("t"), p.cnt["t"])
            s.wait_ge(p.sem("t"), p.cnt["t"])
        for ct in range(NCT):
            iv = v.tensor_copy(tmpb[:, ct, :], src[:, ct, n0:n0 + TCH])
            isq = s.activation(tmpq[:, ct, :], src[:, ct, n0:n0 + TCH], AF.Square)
        vv = p.inc(iv, "v")
        sv = p.inc(isq, "s")
        t.wait_ge(p.sem("v"), vv)
        t.wait_ge(p.sem("s"), sv)
        bk.mm_wait(0)
        bk.mm_wait(1)
        for ct in range(NCT):
            t.matmul(ps_s, ones_b[:, 0:1], tmpb[:, ct, :], start=(ct == 0), stop=(ct == NCT - 1))
        for ct in range(NCT):
            it = t.matmul(ps_q, ones_b[:, 0:1], tmpq[:, ct, :], start=(ct == 0), stop=(ct == NCT - 1))
        tv = p.inc(it, "t")
        bk.wrote(0, ("t", tv))
        bk.wrote(1, ("t", tv))
        bk.rd_wait(v, 0)
        bk.rd_wait(v, 1)
        v.tensor_scalar(rows[:, 0, :], ps_s, 1.0 / D, None, ALU.mult)
        iv = v.tensor_scalar(rows[:, 1, :], ps_q, 1.0 / D, None, ALU.mult)
        bk.read(0, ("v", p.esw(iv, "v")))
        bk.read(1, ("v", p.cnt["v"]))
        iv = v.tensor_mul(rows[:, 2, :], rows[:, 0, :], rows[:, 0, :])
        p.esw(iv, "v")
        iv = v.tensor_sub(rows[:, 3, :], rows[:, 1, :], rows[:, 2, :])
        vv = p.inc(iv, "v")
        s.wait_ge(p.sem("v"), vv)
        isq = s.activation(rows[:, 3, :], rows[:, 3, :], AF.Ln, bias=eps_c)
        p.esw(isq, "s")
        isq = s.activation(rows[:, 3, :], rows[:, 3, :], AF.Exp, scale=p.cc(-0.5, 1))
        sv = p.inc(isq, "s")
        v.wait_ge(p.sem("s"), sv)
        v.tensor_copy(rows_b[:, 0, :], rows[:, 0, :])
        iv = v.tensor_copy(rows_b[:, 1, :], rows[:, 3, :])
        vv = p.inc(iv, "v")
        t.wait_ge(p.sem("v"), vv)
        bk.mm_wait(2)
        bk.mm_wait(3)
        t.matmul(ps_mu[:, :], onesr[:, :], rows_b[:, 0, :], start=True, stop=True)
        it = t.matmul(ps_rs[:, :], onesr[:, :], rows_b[:, 1, :], start=True, stop=True)
        tv = p.inc(it, "t")
        bk.wrote(2, ("t", tv))
        bk.wrote(3, ("t", tv))
        bk.rd_wait(v, 2)
        bk.rd_wait(v, 3)
        for ct in range(NCT):
            iv = v.tensor_sub(tmpb[:, ct, :], src[:, ct, n0:n0 + TCH], ps_mu[:, :])
        p.esw(iv, "v")
        bk.read(2, ("v", p.cnt["v"]))
        for ct in range(NCT):
            if xn_f32 is not None:
                iv = v.tensor_mul(xn_f32[:, ct, n0:n0 + TCH], tmpb[:, ct, :], ps_rs[:, :])
            if scale_cols is None:
                iv = v.tensor_mul(xn_out[:, ct, n0:n0 + TCH], tmpb[:, ct, :], ps_rs[:, :])
            else:
                iv = v.tensor_mul(tmpq[:, ct, :], tmpb[:, ct, :], ps_rs[:, :])
        if scale_cols is not None:
            p.esw(iv, "v")
            for ct in range(NCT):
                iv = v.tensor_scalar(xn_out[:, ct, n0:n0 + TCH], tmpq[:, ct, :],
                                     scale_cols[:, ct:ct + 1], bias_cols[:, ct:ct + 1],
                                     ALU.mult, ALU.add)
        vv = p.inc(iv, "v")
        bk.read(3, ("v", vv))
    sc.close()
    return ("v", p.cnt["v"])



def emit_load_x(p, x_src_hbm, X):
    i = p.nc.sync.dma_start(X[:, :, :], x_src_hbm.rearrange("(ct pp) t -> pp ct t", pp=128))
    return ("qs", p.inc(i, "qs", 16))


def emit_store_x(p, X, dst_hbm, wait_tok=None, q="qs"):
    eng = p.nc.gpsimd if q == "qg" else p.nc.sync
    if wait_tok is not None:
        eng.wait_ge(p.sem(wait_tok[0]), wait_tok[1])
    i = eng.dma_start(dst_hbm.rearrange("(ct pp) t -> pp ct t", pp=128), X[:, :, :])
    return (q, p.inc(i, q, 16))


def finish(p):
    p.es.close()


def run_program(p, n_cores=8):
    in_maps = []
    for c in range(n_cores):
        m = dict(p.host)
        for k, arrs in p.host_percore.items():
            m[k] = arrs[c]
        in_maps.append(m)
    res = run_bass_kernel_spmd(p.nc, in_maps, core_ids=list(range(n_cores)))
    return res.results


def load_w(p, hbm_ap, name, kdim, odim, q="qg", scope=None):
    sbt = p.sb(name + "_sb", [128, kdim // 128, odim], BF16, scope)
    eng = p.nc.gpsimd if q == "qg" else p.nc.sync
    i = eng.dma_start(sbt[:, :, :], hbm_ap.rearrange("(kt pp) o -> pp kt o", pp=128))
    return sbt, (q, p.inc(i, q, 16))



def emit_l0_attn(p, bk, X, XN, wq_h, wk_h, wv_h, wfc_h, gcols, bcols, xn_tok):
    t, v, s, g, sy = p.engines()
    p.barrier("phb")
    scF = contextlib.ExitStack()
    KV = p.sb("l0KV", [128, 8, 64], BF16, scF)
    ksum = p.sb("l0ks", [128, 8], F32, scF)
    sc = contextlib.ExitStack()
    Kf = p.sb("l0K", [128, NTT, D], BF16, sc)
    Vf = p.sb("l0V", [128, NTT, D], BF16, sc)
    wbuf = p.sb("l0wkv", [128, NCT, D], BF16, sc)
    i = g.dma_start(wbuf[:, :, :], wk_h.rearrange("(kt pp) o -> pp kt o", pp=128))
    wk_tok = ("qg", p.inc(i, "qg", 16))
    t.wait_ge(p.sem(xn_tok[0]), xn_tok[1])
    for part, KVf, act in ((0, Kf, AF.Sigmoid), (1, Vf, None)):
        t.wait_ge(p.sem("qg"), p.cnt["qg"])
        for tt in range(NTT):
            for half in range(2):
                b = ((tt * 2 + half) % 4) + part * 4
                bk.mm_wait(b)
                for kt in range(NCT):
                    ik = t.matmul(bk.t[b][:, :],
                                  XN[:, kt, tt * 128:(tt + 1) * 128],
                                  wbuf[:, kt, half * 512:(half + 1) * 512],
                                  start=(kt == 0), stop=(kt == NCT - 1))
                bk.wrote(b, ("t", p.inc(ik, "t")))
                if act is not None:
                    bk.rd_wait(s, b)
                    ie = s.activation(KVf[:, tt, half * 512:(half + 1) * 512],
                                      bk.t[b][:, :], act)
                    bk.read(b, ("s", p.inc(ie, "s")))
                else:
                    bk.rd_wait(v, b)
                    ie = v.tensor_copy(KVf[:, tt, half * 512:(half + 1) * 512], bk.t[b][:, :])
                    bk.read(b, ("v", p.inc(ie, "v")))
        if part == 0:
            g.wait_ge(p.sem("t"), p.cnt["t"])
            i = g.dma_start(wbuf[:, :, :], wv_h.rearrange("(kt pp) o -> pp kt o", pp=128))
            p.inc(i, "qg", 16)
    t.wait_ge(p.sem("s"), p.cnt["s"])
    t.wait_ge(p.sem("v"), p.cnt["v"])
    bk.mm_wait(0)
    bk.mm_wait(1)
    kvps = bk.t[0]
    for h in range(H):
        pb = 64 * (h % 2)
        for tt in range(NTT):
            ik = t.matmul(kvps[pb:pb + 64, (h // 2) * 64:(h // 2) * 64 + 64],
                          Kf[:, tt, h * 64:h * 64 + 64], Vf[:, tt, h * 64:h * 64 + 64],
                          start=(tt == 0), stop=(tt == NTT - 1),
                          tile_position=(0, pb))
    bk.wrote(0, ("t", p.inc(ik, "t")))
    ksps = bk.t[1]
    onesb = p.C["ones_col_b"]
    for ct in range(NCT):
        for tt in range(NTT):
            ik = t.matmul(ksps[:, ct:ct + 1], Kf[:, tt, ct * 128:(ct + 1) * 128],
                          onesb[:, 0:1], start=(tt == 0), stop=(tt == NTT - 1))
    bk.wrote(1, ("t", p.inc(ik, "t")))
    bk.rd_wait(v, 0)
    ivv = v.tensor_copy(KV[:, :, :], kvps[:, :].rearrange("p (a b) -> p a b", a=8))
    bk.read(0, ("v", p.inc(ivv, "v")))
    bk.rd_wait(v, 1)
    ivv = v.tensor_copy(ksum[:, :], ksps[:, 0:8])
    bk.read(1, ("v", p.inc(ivv, "v")))
    sc.close()

    p.barrier("l0bar")

    sc2 = contextlib.ExitStack()
    wq, wq_tok = load_w(p, wq_h, "l0wq", D, D, scope=sc2)
    wfc, wfc_tok = load_w(p, wfc_h, "l0wfc", D, D, scope=sc2)
    Qb = p.sb("l0Q", [128, NCT, TCH], BF16, sc2)
    tbA = p.sb("l0tA", [128, NCT, TCH], BF16, sc2)
    tbB = p.sb("l0tB", [128, NCT, TCH], BF16, sc2)
    X2 = p.sb("l0X2", [128, NCT, TCH], F32, sc2)
    rows = p.sb("l0rw", [16, 3, TCH], F32, sc2)
    rows_b = p.sb("l0rb", [16, TCH], BF16, sc2)
    lrows = p.sb("l0lr", [1, 4, TCH], F32, sc2)
    lrows_b = p.sb("l0lrb", [1, 2, TCH], BF16, sc2)
    hpsel = p.C["hpsel"]
    bc16 = p.C["bc16"]
    onesr = p.C["ones_row_b"]
    t.wait_ge(p.sem("qg"), wfc_tok[1])
    for c in range(NCH):
        n0 = c * TCH
        if c > 0:
            v.wait_ge(p.sem("v"), p.cnt["v"])
            s.wait_ge(p.sem("v"), p.cnt["v"])
        for ct in range(NCT):
            b = ct % 2
            bk.mm_wait(b)
            for kt in range(NCT):
                ik = t.matmul(bk.t[b][:, :], wq[:, kt, ct * 128:(ct + 1) * 128],
                              XN[:, kt, n0:n0 + TCH], start=(kt == 0), stop=(kt == NCT - 1))
            bk.wrote(b, ("t", p.inc(ik, "t")))
            bk.rd_wait(s, b)
            isg = s.activation(Qb[:, ct, :], bk.t[b][:, :], AF.Sigmoid)
            bk.read(b, ("s", p.inc(isg, "s")))
        sv = p.cnt["s"]
        v.wait_ge(p.sem("s"), sv)
        for ct in range(NCT):
            iv = v.tensor_scalar(tbA[:, ct, :], Qb[:, ct, :], ksum[:, ct:ct + 1], None, ALU.mult)
        vv = p.inc(iv, "v")
        t.wait_ge(p.sem("v"), vv)
        bk.mm_wait(2)
        for ct in range(NCT):
            ik = t.matmul(bk.t[2][0:16, :], hpsel[:, ct, :], tbA[:, ct, :],
                          start=(ct == 0), stop=(ct == NCT - 1))
        bk.wrote(2, ("t", p.inc(ik, "t")))
        bk.rd_wait(s, 2)
        isg = s.activation(rows[:, 0, :], bk.t[2][0:16, :], AF.Ln, bias=p.cc(1e-8, 16))
        bk.read(2, ("s", p.esw(isg, "s")))
        isg = s.activation(rows[:, 1, :], rows[:, 0, :], AF.Exp, scale=p.cc(-1.0, 16))
        sv = p.inc(isg, "s")
        v.wait_ge(p.sem("s"), sv)
        iv = v.tensor_copy(rows_b[:, :], rows[:, 1, :])
        vv = p.inc(iv, "v")
        t.wait_ge(p.sem("v"), vv)
        for ct in range(NCT):
            b = 3 + ct % 2
            bk.mm_wait(b)
            for hh in range(2):
                h = 2 * ct + hh
                pb = 64 * hh
                ik = t.matmul(bk.t[b][pb:pb + 64, :], KV[pb:pb + 64, ct, :],
                              Qb[pb:pb + 64, ct, :], start=True, stop=True,
                              tile_position=(pb, pb))
            bk.wrote(b, ("t", p.inc(ik, "t")))
            bk.rd_wait(s, b)
            ie = s.activation(tbA[:, ct, :], bk.t[b][:, :], AF.Identity)
            sv = p.inc(ie, "s")
            bk.read(b, ("s", sv))
            b5 = 5 if ct % 2 == 0 else 6
            bk.mm_wait(b5)
            ik = t.matmul(bk.t[b5][:, :], bc16[:, ct, :], rows_b[:, :], start=True, stop=True)
            bk.wrote(b5, ("t", p.inc(ik, "t")))
            bk.rd_wait(v, b5)
            v.wait_ge(p.sem("s"), sv)
            iv = v.tensor_mul(tbB[:, ct, :], tbA[:, ct, :], bk.t[b5][:, :])
            bk.read(b5, ("v", p.inc(iv, "v")))
        vv = p.cnt["v"]
        t.wait_ge(p.sem("v"), vv)
        for ct in range(NCT):
            b = ct % 2
            bk.mm_wait(b)
            for kt in range(NCT):
                ik = t.matmul(bk.t[b][:, :], wfc[:, kt, ct * 128:(ct + 1) * 128],
                              tbB[:, kt, :], start=(kt == 0), stop=(kt == NCT - 1))
            bk.wrote(b, ("t", p.inc(ik, "t")))
            bk.rd_wait(v, b)
            iv = v.tensor_add(X2[:, ct, :], XN[:, ct, n0:n0 + TCH], bk.t[b][:, :])
            bk.read(b, ("v", p.inc(iv, "v")))
        v.wait_ge(p.sem("v"), p.cnt["v"])
        vv = p.inc(v.nop(), "v")
        s.wait_ge(p.sem("v"), vv)
        for ct in range(NCT):
            iv = v.tensor_copy(tbA[:, ct, :], X2[:, ct, :])
            isq = s.activation(tbB[:, ct, :], X2[:, ct, :], AF.Square)
        vv = p.inc(iv, "v")
        sv = p.inc(isq, "s")
        t.wait_ge(p.sem("v"), vv)
        t.wait_ge(p.sem("s"), sv)
        bk.mm_wait(2)
        bk.mm_wait(7)
        for ct in range(NCT):
            t.matmul(bk.t[2][0:1, :], onesb[:, 0:1], tbA[:, ct, :],
                     start=(ct == 0), stop=(ct == NCT - 1))
        for ct in range(NCT):
            ik = t.matmul(bk.t[7][0:1, :], onesb[:, 0:1], tbB[:, ct, :],
                          start=(ct == 0), stop=(ct == NCT - 1))
        tv = p.inc(ik, "t")
        bk.wrote(2, ("t", tv))
        bk.wrote(7, ("t", tv))
        bk.rd_wait(v, 2)
        bk.rd_wait(v, 7)
        v.tensor_scalar(lrows[:, 0, :], bk.t[2][0:1, :], 1.0 / D, None, ALU.mult)
        iv = v.tensor_scalar(lrows[:, 1, :], bk.t[7][0:1, :], 1.0 / D, None, ALU.mult)
        bk.read(2, ("v", p.esw(iv, "v")))
        bk.read(7, ("v", p.cnt["v"]))
        iv = v.tensor_mul(lrows[:, 2, :], lrows[:, 0, :], lrows[:, 0, :])
        p.esw(iv, "v")
        iv = v.tensor_sub(lrows[:, 3, :], lrows[:, 1, :], lrows[:, 2, :])
        vv = p.inc(iv, "v")
        s.wait_ge(p.sem("v"), vv)
        isg = s.activation(lrows[:, 3, :], lrows[:, 3, :], AF.Ln, bias=p.cc(1e-5, 1))
        p.esw(isg, "s")
        isg = s.activation(lrows[:, 3, :], lrows[:, 3, :], AF.Exp, scale=p.cc(-0.5, 1))
        sv = p.inc(isg, "s")
        v.wait_ge(p.sem("s"), sv)
        v.tensor_copy(lrows_b[:, 0, :], lrows[:, 0, :])
        iv = v.tensor_copy(lrows_b[:, 1, :], lrows[:, 3, :])
        vv = p.inc(iv, "v")
        t.wait_ge(p.sem("v"), vv)
        bk.mm_wait(2)
        bk.mm_wait(7)
        t.matmul(bk.t[2][:, :], onesr[:, :], lrows_b[:, 0, :], start=True, stop=True)
        ik = t.matmul(bk.t[7][:, :], onesr[:, :], lrows_b[:, 1, :], start=True, stop=True)
        tv = p.inc(ik, "t")
        bk.wrote(2, ("t", tv))
        bk.wrote(7, ("t", tv))
        bk.rd_wait(v, 2)
        bk.rd_wait(v, 7)
        for ct in range(NCT):
            iv = v.tensor_sub(tbA[:, ct, :], X2[:, ct, :], bk.t[2][:, :])
        p.esw(iv, "v")
        bk.read(2, ("v", p.cnt["v"]))
        for ct in range(NCT):
            iv = v.tensor_mul(tbB[:, ct, :], tbA[:, ct, :], bk.t[7][:, :])
        p.esw(iv, "v")
        bk.read(7, ("v", p.cnt["v"]))
        for ct in range(NCT):
            iv = v.tensor_scalar(tbA[:, ct, :], tbB[:, ct, :], gcols[:, ct:ct + 1],
                                 bcols[:, ct:ct + 1], ALU.mult, ALU.add)
        p.esw(iv, "v")
        for ct in range(NCT):
            iv = v.tensor_add(X[:, ct, n0:n0 + TCH], X[:, ct, n0:n0 + TCH], tbA[:, ct, :])
        vv = p.inc(iv, "v")
    sc2.close()
    scF.close()
    return ("v", p.cnt["v"])



def emit_l0_ffn(p, bk, X, XN, router_h, weg_h, weu_h, wed_h, wsg_h, wsu_h, wsd_h, xn_tok,
                coeft_out=None, stop_after=None):
    t, v, s, g, sy = p.engines()
    p.barrier("phb")
    sc = contextlib.ExitStack()
    E = 8
    rtr, rtr_tok = load_w(p, router_h, "rtr", D, E, scope=sc)
    COEFT = p.sb("coeft", [128, NTT, E], F32, sc)
    COEFF = p.sb("coeff", [E, T], F32, sc)
    LGs = p.sb("lgs", [128, NTT, E], F32, sc)
    NM1 = p.sb("nm1", [128, 4, NTT], F32, sc)
    D3 = p.sb("d3", [128, NTT, E], F32, sc)
    E3 = p.sb("e3", [128, NTT, E], F32, sc)
    I3 = p.sb("i3", [128, NTT, E], F32, sc)
    t.wait_ge(p.sem(xn_tok[0]), xn_tok[1])
    t.wait_ge(p.sem("qg"), rtr_tok[1])
    for tt in range(NTT):
        b = tt % 2
        bk.mm_wait(b)
        for kt in range(NCT):
            ik = t.matmul(bk.t[b][:, 0:E], XN[:, kt, tt * 128:(tt + 1) * 128],
                          rtr[:, kt, :], start=(kt == 0), stop=(kt == NCT - 1))
        bk.wrote(b, ("t", p.inc(ik, "t")))
        bk.rd_wait(v, b)
        iv = v.tensor_copy(LGs[:, tt, :], bk.t[b][:, 0:E])
        bk.read(b, ("v", p.inc(iv, "v")))

    def vsw(inst):
        vv = p.inc(inst, "v")
        v.wait_ge(p.sem("v"), vv)

    lg3 = LGs[:, :, :]
    v.wait_ge(p.sem("v"), p.cnt["v"])
    vsw(v.reduce_max(NM1[:, 0, :], lg3, axis=AX.X, negate=True))
    a1, a2 = broadcast_tensor_aps(lg3, NM1[:, 0, :, None])
    iv = v.tensor_tensor(D3[:, :, :], a1, a2, ALU.add)
    sv = p.inc(iv, "v")
    s.wait_ge(p.sem("v"), sv)
    isg = s.activation(E3[:, :, :], D3[:, :, :], AF.Exp)
    sv = p.inc(isg, "s")
    v.wait_ge(p.sem("s"), sv)
    vsw(v.reduce_sum(NM1[:, 1, :], E3[:, :, :], axis=AX.X))
    vsw(v.reciprocal(NM1[:, 2, :], NM1[:, 1, :]))
    vsw(v.tensor_scalar(I3[:, :, :], D3[:, :, :], 0.0, None, ALU.is_ge))
    vsw(v.scalar_tensor_tensor(D3[:, :, :], I3[:, :, :], -1e9, D3[:, :, :], ALU.mult, ALU.add))
    vsw(v.reduce_max(NM1[:, 3, :], D3[:, :, :], axis=AX.X))
    a1, a2 = broadcast_tensor_aps(D3[:, :, :], NM1[:, 3, :, None])
    vsw(v.tensor_tensor(D3[:, :, :], a1, a2, ALU.is_ge))
    vsw(v.tensor_add(I3[:, :, :], I3[:, :, :], D3[:, :, :]))
    vsw(v.tensor_mul(I3[:, :, :], I3[:, :, :], E3[:, :, :]))
    a1, a2 = broadcast_tensor_aps(I3[:, :, :], NM1[:, 2, :, None])
    vsw(v.tensor_tensor(COEFT[:, :, :], a1, a2, ALU.mult))
    t.wait_ge(p.sem("v"), p.cnt["v"])
    idn = p.C["identity"]
    for bb in range(4):
        bk.mm_wait(2 + bb)
    for tt in range(NTT):
        ik = t.transpose(bk.t[2 + tt // 4][0:E, (tt % 4) * 128:(tt % 4) * 128 + 128],
                         COEFT[:, tt, :], idn[:, :])
    tv = p.inc(ik, "t")
    for bb in range(4):
        bk.wrote(2 + bb, ("t", tv))
        bk.rd_wait(v, 2 + bb)
        iv = v.tensor_copy(COEFF[:, bb * 512:(bb + 1) * 512], bk.t[2 + bb][0:E, 0:512])
        bk.read(2 + bb, ("v", p.inc(iv, "v")))
    if stop_after == "coef":
        if coeft_out is not None:
            iv = v.tensor_copy(coeft_out[:, 0:16, :], COEFT[:, :, :])
            iv = v.tensor_copy(coeft_out[:, 16:32, :], LGs[:, :, :])
            iv = v.tensor_copy(coeft_out[:, 32:40, :], NM1[:, 0:4, :].rearrange("p a b -> p (a b)").rearrange("p (a e) -> p a e", e=8))
            iv = v.tensor_copy(coeft_out[:, 40:48, :], rtr[:, :, :])
            p.inc(iv, "v")
        sc.close()
        return ("v", p.cnt["v"])
    onesf = p.C["ones_row_f"]
    crow = p.sb("crow", [1, 2, T], F32, sc)
    cscr = p.dram(f"coefscr{p.uid}", [E, T], F32)
    p.uid += 1
    g.wait_ge(p.sem("v"), p.cnt["v"])
    i = g.dma_start(cscr, COEFF[:, :])
    p.inc(i, "qg", 16)

    def load_crow(e, i2):
        qn = f"qe{i2}"
        ii = g.dma_start(crow[:, i2, :], cscr[e:e + 1, :])
        return (qn, p.inc(ii, qn, 16))

    Gb = p.sb("moeg", [128, 8, TCH], BF16, sc)
    Ub = p.sb("moeu", [128, 8, TCH], BF16, sc)
    Hb = p.sb("moeh", [128, 8, TCH], BF16, sc)

    def path(wg, wu, wd, dt, coef_row):
        for c in range(NCH):
            n0 = c * TCH
            s.wait_ge(p.sem("v"), p.mul_tok) if False else None
            s.wait_ge(p.sem("v"), p.cnt["v"])
            v.wait_ge(p.sem("t"), p.cnt["t"])
            for mt in range(dt):
                b = mt % 2
                bk.mm_wait(b)
                for kt in range(NCT):
                    ik = t.matmul(bk.t[b][:, :], wg[:, kt, mt * 128:(mt + 1) * 128],
                                  XN[:, kt, n0:n0 + TCH], start=(kt == 0), stop=(kt == NCT - 1))
                bk.wrote(b, ("t", p.inc(ik, "t")))
                bk.rd_wait(s, b)
                ie = s.activation(Gb[:, mt, :], bk.t[b][:, :], AF.Silu)
                bk.read(b, ("s", p.inc(ie, "s")))
                b2 = 2 + mt % 2
                bk.mm_wait(b2)
                for kt in range(NCT):
                    ik = t.matmul(bk.t[b2][:, :], wu[:, kt, mt * 128:(mt + 1) * 128],
                                  XN[:, kt, n0:n0 + TCH], start=(kt == 0), stop=(kt == NCT - 1))
                bk.wrote(b2, ("t", p.inc(ik, "t")))
                bk.rd_wait(v, b2)
                ie = v.tensor_copy(Ub[:, mt, :], bk.t[b2][:, :])
                bk.read(b2, ("v", p.inc(ie, "v")))
            if coef_row is not None:
                bk.mm_wait(4)
                ik = t.matmul(bk.t[4][:, :], onesf[:, :], coef_row[:, n0:n0 + TCH],
                              start=True, stop=True)
                bk.wrote(4, ("t", p.inc(ik, "t")))
                bk.rd_wait(v, 4)
            sv = p.cnt["s"]
            v.wait_ge(p.sem("s"), sv)
            v.wait_ge(p.sem("v"), p.cnt["v"])
            for mt in range(dt):
                iv = v.tensor_mul(Hb[:, mt, :], Gb[:, mt, :], Ub[:, mt, :])
            if coef_row is not None:
                p.esw(iv, "v")
                for mt in range(dt):
                    iv = v.tensor_mul(Hb[:, mt, :], Hb[:, mt, :], bk.t[4][:, :])
            vv = p.esw(iv, "v")
            if coef_row is not None:
                bk.read(4, ("v", vv))
            t.wait_ge(p.sem("v"), vv)
            for mt2 in range(NCT):
                b3 = 5 + mt2 % 2
                bk.mm_wait(b3)
                for kt in range(dt):
                    ik = t.matmul(bk.t[b3][:, :], wd[:, kt, mt2 * 128:(mt2 + 1) * 128],
                                  Hb[:, kt, :], start=(kt == 0), stop=(kt == dt - 1))
                bk.wrote(b3, ("t", p.inc(ik, "t")))
                bk.rd_wait(v, b3)
                if mt2 == 0 and c == NCH - 1 and hasattr(p, "_dps"):
                    v.tensor_copy(p._dps[:, 0, :], bk.t[b3][:, :])
                    v.tensor_copy(p._dps[:, 1, :], Hb[:, 0, :])
                    v.tensor_copy(p._dps[:, 2, :], Hb[:, 7, :])
                iv = v.tensor_add(X[:, mt2, n0:n0 + TCH], X[:, mt2, n0:n0 + TCH], bk.t[b3][:, :])
                bk.read(b3, ("v", p.inc(iv, "v")))

    scS = contextlib.ExitStack()
    wsg, _ = load_w(p, wsg_h, "wsg", D, D, scope=scS)
    wsu, _ = load_w(p, wsu_h, "wsu", D, D, scope=scS)
    wsd, tok = load_w(p, wsd_h, "wsd", D, D, scope=scS)
    t.wait_ge(p.sem("qg"), tok[1])
    path(wsg, wsu, wsd, NCT, None)
    p.barrier("moebar")
    if stop_after == "shared" and hasattr(p, "_moedbg"):
        iv = v.tensor_copy(p._moedbg[:, 0, :], Gb[:, 0, 0:TCH])
        iv = v.tensor_copy(p._moedbg[:, 1, :], Ub[:, 0, 0:TCH])
        iv = v.tensor_copy(p._moedbg[:, 2, :], Hb[:, 0, 0:TCH])
        p.inc(iv, "v")
    if stop_after == "shared":
        scS.close()
        sc.close()
        return ("v", p.cnt["v"])
    scS.close()
    scE = contextlib.ExitStack()
    slots = []
    for i in range(2):
        slots.append((p.sb(f"ewg{i}", [128, 8, 512], BF16, scE),
                      p.sb(f"ewu{i}", [128, 8, 512], BF16, scE),
                      p.sb(f"ewd{i}", [128, 4, D], BF16, scE)))
    slot_free = [None, None]

    def load_expert(e, i):
        wg, wu, wd = slots[i]
        qn = f"qe{i}"
        if slot_free[i] is not None:
            g.wait_ge(p.sem(slot_free[i][0]), slot_free[i][1])
        p.inc(g.dma_start(wg[:, :, :], weg_h[e].rearrange("(kt pp) o -> pp kt o", pp=128)), qn, 16)
        p.inc(g.dma_start(wu[:, :, :], weu_h[e].rearrange("(kt pp) o -> pp kt o", pp=128)), qn, 16)
        i3 = g.dma_start(wd[:, :, :], wed_h[e].rearrange("(kt pp) o -> pp kt o", pp=128))
        return (qn, p.inc(i3, qn, 16))

    ldtok = [load_expert(0, 0), load_expert(1, 1)]
    crtok = [load_crow(0, 0), load_crow(1, 1)]
    for e in range(8):
        i = e % 2
        t.wait_ge(p.sem(crtok[i][0]), crtok[i][1])
        wg, wu, wd = slots[i]
        path(wg, wu, wd, 4, crow[0:1, i, :])
        slot_free[i] = ("t", p.cnt["t"])
        if e + 2 < 8:
            ldtok[i] = load_expert(e + 2, i)
            crtok[i] = load_crow(e + 2, i)
    p.barrier("moebar2")
    scE.close()
    sc.close()
    return ("v", p.cnt["v"])



def emit_l1_attn(p, bk, X, XN, w_h, bc4, sc_cols, xn_tok):
    t, v, s, g, sy = p.engines()
    p.barrier("phb")
    sc = contextlib.ExitStack()
    H8 = p.sb("l1aH", [128, NCT, T], BF16, sc)
    BI = p.sb("l1ai", [128, T], F32, sc)
    BF = p.sb("l1af", [128, T], F32, sc)
    BZ = p.sb("l1az", [128, T], F32, sc)
    B1 = p.sb("l1a1", [128, T], F32, sc)
    BM = p.sb("l1am", [128, T], F32, sc)
    BP = p.sb("l1ap", [128, T], F32, sc)
    rows = p.sb("l1ar", [1, T], F32, sc)
    rows_b = p.sb("l1arb", [1, T], BF16, sc)
    wsl = [p.sb("l1aw0", [128, NCT, 4, 128], BF16, sc)]
    onesb = p.C["ones_col_b"]
    onesr = p.C["ones_row_b"]

    def load_slice(ct):
        ii = g.dma_start(wsl[0][:, :, :, :],
                         w_h[:, ct, :, :].rearrange("(kt pp) gt o -> pp kt gt o", pp=128))
        return ("qa0", p.inc(ii, "qa0", 16))

    ltok = load_slice(0)
    t.wait_ge(p.sem(xn_tok[0]), xn_tok[1])
    for ct in range(NCT):
        t.wait_ge(p.sem(ltok[0]), ltok[1])
        w = wsl[0]
        for c in range(NCH):
            n0 = c * TCH
            for gt in range(4):
                bk.mm_wait(gt)
                for kt in range(NCT):
                    ik = t.matmul(bk.t[gt][:, :], w[:, kt, gt, :], XN[:, kt, n0:n0 + TCH],
                                  start=(kt == 0), stop=(kt == NCT - 1))
                bk.wrote(gt, ("t", p.inc(ik, "t")))
            bk.rd_wait(s, 0)
            ie = s.activation(BI[:, n0:n0 + TCH], bk.t[0][:, :], AF.Tanh,
                              scale=p.cc(0.1), bias=bc4[:, 0, ct:ct + 1])
            bk.read(0, ("s", p.inc(ie, "s")))
            bk.rd_wait(s, 1)
            ie = s.activation(BF[:, n0:n0 + TCH], bk.t[1][:, :], AF.Exp,
                              bias=bc4[:, 1, ct:ct + 1])
            bk.read(1, ("s", p.esw(ie, "s")))
            ie = s.activation(BF[:, n0:n0 + TCH], BF[:, n0:n0 + TCH], AF.Ln,
                              bias=p.cc(1.0))
            p.esw(ie, "s")
            bk.rd_wait(s, 2)
            ie = s.activation(H8[:, ct, n0:n0 + TCH], bk.t[2][:, :], AF.Sigmoid,
                              bias=bc4[:, 2, ct:ct + 1])
            bk.read(2, ("s", p.inc(ie, "s")))
            bk.rd_wait(s, 3)
            ie = s.activation(BZ[:, n0:n0 + TCH], bk.t[3][:, :], AF.Tanh,
                              bias=bc4[:, 3, ct:ct + 1])
            bk.read(3, ("s", p.inc(ie, "s")))
        if ct + 1 < NCT:
            g.wait_ge(p.sem("t"), p.cnt["t"])
            ltok = load_slice(ct + 1)
        sv = p.cnt["s"]
        v.wait_ge(p.sem("s"), sv)
        iv = v.tensor_scalar(BI[:, :], BI[:, :], 10.0, None, ALU.mult)
        iv2 = v.tensor_scalar(BF[:, :], BF[:, :], -1.0, None, ALU.mult)
        p.esw(iv2, "v")
        iv = v.tensor_tensor_scan(BM[:, :], BF[:, :], BI[:, :], 0.0, ALU.add, ALU.max)
        p.esw(iv, "v")
        v.memset(BP[:, 0:1], 0.0)
        iv = v.tensor_copy(BP[:, 1:T], BM[:, 0:T - 1])
        p.esw(iv, "v")
        iv = v.tensor_sub(B1[:, :], BI[:, :], BM[:, :])
        iv2 = v.tensor_sub(BP[:, :], BP[:, :], BM[:, :])
        p.esw(iv2, "v")
        iv = v.tensor_add(BF[:, :], BF[:, :], BP[:, :])
        vv = p.inc(iv, "v")
        s.wait_ge(p.sem("v"), vv)
        ie = s.activation(B1[:, :], B1[:, :], AF.Exp)
        ie2 = s.activation(BF[:, :], BF[:, :], AF.Exp)
        sv = p.inc(ie2, "s")
        v.wait_ge(p.sem("s"), sv)
        iv = v.tensor_mul(BP[:, :], B1[:, :], BZ[:, :])
        p.esw(iv, "v")
        iv = v.tensor_tensor_scan(BI[:, :], BF[:, :], BP[:, :], 0.0, ALU.mult, ALU.add)
        iv = v.tensor_tensor_scan(BZ[:, :], BF[:, :], B1[:, :], 0.0, ALU.mult, ALU.add)
        vv = p.inc(iv, "v")
        s.wait_ge(p.sem("v"), vv)
        ie = s.activation(BF[:, :], BZ[:, :], AF.Ln, bias=p.cc(1e-6))
        p.esw(ie, "s")
        ie = s.activation(BF[:, :], BF[:, :], AF.Exp, scale=p.cc(-1.0))
        sv = p.inc(ie, "s")
        v.wait_ge(p.sem("s"), sv)
        iv = v.tensor_mul(BM[:, :], BI[:, :], BF[:, :])
        p.esw(iv, "v")
        iv = v.tensor_mul(H8[:, ct, :], BM[:, :], H8[:, ct, :])
        vv = p.esw(iv, "v")
        s.wait_ge(p.sem("v"), vv)
        ie = s.activation(B1[:, :], H8[:, ct, :], AF.Square)
        sv = p.inc(ie, "s")
        t.wait_ge(p.sem("s"), sv)
        for c in range(NCH):
            bk.mm_wait(4 + c)
            ik = t.matmul(bk.t[4 + c][0:1, :], p.C["ones_col_f"][:, 0:1],
                          B1[:, c * TCH:(c + 1) * TCH], start=True, stop=True)
            bk.wrote(4 + c, ("t", p.inc(ik, "t")))
        if ct == 0:
            iv = v.memset(rows[:, :], 0.0)
            p.esw(iv, "v")
        for c in range(NCH):
            bk.rd_wait(v, 4 + c)
            iv = v.tensor_add(rows[:, c * TCH:(c + 1) * TCH],
                              rows[:, c * TCH:(c + 1) * TCH], bk.t[4 + c][0:1, :])
            bk.read(4 + c, ("v", p.esw(iv, "v")))
    iv = v.tensor_scalar(rows[:, :], rows[:, :], 1.0 / D, None, ALU.mult)
    p.esw(iv, "v")
    vv = p.cnt["v"]
    s.wait_ge(p.sem("v"), vv)
    ie = s.activation(rows[:, :], rows[:, :], AF.Ln, bias=p.cc(1e-8, 1))
    p.esw(ie, "s")
    ie = s.activation(rows[:, :], rows[:, :], AF.Exp, scale=p.cc(-0.5, 1))
    sv = p.inc(ie, "s")
    v.wait_ge(p.sem("s"), sv)
    iv = v.tensor_copy(rows_b[:, :], rows[:, :])
    vv = p.inc(iv, "v")
    t.wait_ge(p.sem("v"), vv)
    for c in range(NCH):
        n0 = c * TCH
        bk.mm_wait(c)
        ik = t.matmul(bk.t[c][:, :], onesr[:, :], rows_b[:, n0:n0 + TCH],
                      start=True, stop=True)
        bk.wrote(c, ("t", p.inc(ik, "t")))
    for c in range(NCH):
        n0 = c * TCH
        bk.rd_wait(v, c)
        for ct in range(NCT):
            iv = v.tensor_mul(BI[:, 0:TCH], H8[:, ct, n0:n0 + TCH], bk.t[c][:, :])
            p.esw(iv, "v")
            iv = v.scalar_tensor_tensor(X[:, ct, n0:n0 + TCH], BI[:, 0:TCH],
                                        sc_cols[:, ct:ct + 1], X[:, ct, n0:n0 + TCH],
                                        ALU.mult, ALU.add)
            p.esw(iv, "v")
        bk.read(c, ("v", p.cnt["v"]))
    sc.close()
    return ("v", p.cnt["v"])



W1P = 5632
NA = 22
ATTN_LEVEL = 99


def emit_l1_ffn_layer(p, bk, li, XLOC, wqkv_h, wo_h, w1_h, w2_h, tabs, qscols, kscols,
                      qsscols, ksscols, lngb, b1c, b2c, parts=("attn", "ffn")):
    t, v, s, g, sy = p.engines()
    p.barrier("phb")
    if "attn" not in parts:
        return _ffn_only(p, bk, li, XLOC, w1_h, w2_h, lngb, b1c, b2c)
    sc = contextlib.ExitStack()
    XBU = p.sb(f"f{li}xbu", [128, 4, T], BF16, sc)
    XBS = p.sb(f"f{li}xbs", [128, 4, T + 2], BF16, sc)
    KN = p.sb(f"f{li}kn", [128, NCT, 8 * 128], BF16, sc)
    VB = p.sb(f"f{li}vb", [128, 8, D], BF16, sc)
    QN = p.sb(f"f{li}qn", [128, NCT, TCH], BF16, sc)
    QR = p.sb(f"f{li}qr", [128, NCT, TCH], BF16, sc)
    Q2 = p.sb(f"f{li}q2", [128, NCT, TCH], BF16, sc)
    CSQ = p.sb(f"f{li}csq", [128, 4, TCH], BF16, sc)
    wqkv, _ = load_w(p, wqkv_h, f"f{li}wqkv", D, 3072, scope=sc)
    wo, _ = load_w(p, wo_h, f"f{li}wo", D, D, scope=sc)
    rws = p.sb(f"f{li}rw", [16, 3, TCH], F32, sc)
    rwb = p.sb(f"f{li}rwb", [16, TCH], BF16, sc)
    NMC = p.sb(f"f{li}nm", [128, 2, 16], F32, sc)
    LSR = p.sb(f"f{li}lsr", [16, 128], F32, sc)
    PRT = p.sb(f"f{li}prt", [128, 384], BF16, sc)
    AVO = p.sb(f"f{li}avo", [128, NCT, TCH], BF16, sc)
    XLC = p.sb(f"f{li}xlc", [128, 2, TCH], F32, sc)
    hp16 = p.C["hpsel"]; bc16 = p.C["bc16"]; idn = p.C["identity"]
    negi = p.C["negi"]; negsel = p.C["negsel"]
    tq0, tq2 = p.C["tri_q0"], p.C["tri_q2"]; tt0, tt2 = p.C["tri_t0"], p.C["tri_t2"]
    xloc_fm = XLOC.rearrange("(ct pp) t -> pp ct t", pp=128)

    iv = v.memset(XBS[:, :, 0:1], 0.0)
    p.esw(iv, "v")
    g.wait_ge(p.sem("qf"), p.cnt.get("qf", 0))
    ig = g.dma_start(XBU[:, :, :], xloc_fm[:, 0:4, :])
    p.inc(ig, "qf", 16)
    ig = g.dma_start(XBS[:, :, 1:T + 1], xloc_fm[:, 4:8, 0:T])
    xb_tok = ("qf", p.inc(ig, "qf", 16))
    v.wait_ge(p.sem("qs"), p.cnt["qs"])
    v.wait_ge(p.sem("qg"), p.cnt["qg"])
    t.wait_ge(p.sem(xb_tok[0]), xb_tok[1])
    t.wait_ge(p.sem("v"), p.cnt["v"])
    t.wait_ge(p.sem("qg"), p.cnt["qg"])

    def qkv_norm_rot(u, part, dst_fm, n0, depth=3):
        cs = CSQ[:, 0 if part == 0 else 2, :]
        sn = CSQ[:, 1 if part == 0 else 3, :]
        if depth < 1:
            return
        for mt in range(NCT):
            b = mt % 2
            bk.mm_wait(b)
            for kt in range(NCT):
                rhs = (XBU[:, kt, u * TCH:u * TCH + TCH] if kt < 4
                       else XBS[:, kt - 4, u * TCH:u * TCH + TCH])
                ik = t.matmul(bk.t[b][:, :],
                              wqkv[:, kt, part * D + mt * 128:part * D + (mt + 1) * 128],
                              rhs, start=(kt == 0), stop=(kt == NCT - 1))
            bk.wrote(b, ("t", p.inc(ik, "t")))
            bk.rd_wait(v, b)
            iv = v.tensor_copy(QR[:, mt, :], bk.t[b][:, :])
            vv = p.inc(iv, "v")
            bk.read(b, ("v", vv))
            s.wait_ge(p.sem("v"), vv)
            ie = s.activation(Q2[:, mt, :], bk.t[b][:, :], AF.Square)
            bk.read(b, ("s", p.inc(ie, "s")))
        if depth < 2:
            return
        sv = p.cnt["s"]
        t.wait_ge(p.sem("s"), sv)
        bk.mm_wait(4)
        for mt in range(NCT):
            ik = t.matmul(bk.t[4][0:16, :], hp16[:, mt, :], Q2[:, mt, :],
                          start=(mt == 0), stop=(mt == NCT - 1))
        bk.wrote(4, ("t", p.inc(ik, "t")))
        bk.rd_wait(s, 4)
        ie = s.activation(rws[:, 0, :], bk.t[4][0:16, :], AF.Sqrt)
        bk.read(4, ("s", p.esw(ie, "s")))
        sv = p.cnt["s"]
        v.wait_ge(p.sem("s"), sv)
        iv = v.tensor_scalar(rws[:, 1, :], rws[:, 0, :], 1e-12, None, ALU.max)
        vv = p.inc(iv, "v")
        s.wait_ge(p.sem("v"), vv)
        ie = s.activation(rws[:, 2, :], rws[:, 1, :], AF.Ln)
        p.esw(ie, "s")
        ie = s.activation(rws[:, 2, :], rws[:, 2, :], AF.Exp, scale=p.cc(-1.0, 16))
        sv = p.inc(ie, "s")
        v.wait_ge(p.sem("s"), sv)
        iv = v.tensor_copy(rwb[:, :], rws[:, 2, :])
        vv = p.inc(iv, "v")
        if depth < 3:
            return
        t.wait_ge(p.sem("v"), vv)
        for mt in range(NCT):
            bk.mm_wait(2)
            ik = t.matmul(bk.t[2][:, :], p.C["psig"][:, :], QR[:, mt, :], start=True, stop=True)
            bk.wrote(2, ("t", p.inc(ik, "t")))
            bk.mm_wait(3)
            ik = t.matmul(bk.t[3][:, :], bc16[:, mt, :], rwb[:, :], start=True, stop=True)
            bk.wrote(3, ("t", p.inc(ik, "t")))
            bk.rd_wait(v, 2)
            bk.rd_wait(v, 3)
            dst = dst_fm[:, mt, n0:n0 + TCH]
            iv = v.tensor_mul(dst, QR[:, mt, :], cs)
            iv2 = v.tensor_mul(Q2[:, mt, :], bk.t[2][:, :], sn)
            bk.read(2, ("v", p.esw(iv2, "v")))
            iv = v.tensor_add(dst, dst, Q2[:, mt, :])
            p.esw(iv, "v")
            iv = v.tensor_mul(dst, dst, bk.t[3][:, :])
            bk.read(3, ("v", p.inc(iv, "v")))

    for u in range(NCH) if ATTN_LEVEL >= 1 else []:
        n0 = u * TCH
        un = u * TCH
        v.wait_ge(p.sem("t"), p.cnt["t"])
        iv = v.tensor_scalar(CSQ[:, 0, :], tabs["cs"][:, un:un + TCH], qscols[:, li:li + 1], 8.0, ALU.mult, ALU.mult)
        iv = v.tensor_scalar(CSQ[:, 1, :], tabs["snp"][:, un:un + TCH], qsscols[:, li:li + 1], 8.0, ALU.mult, ALU.mult)
        iv = v.tensor_scalar(CSQ[:, 2, :], tabs["ci"][:, un:un + TCH], kscols[:, li:li + 1], None, ALU.mult)
        iv = v.tensor_scalar(CSQ[:, 3, :], tabs["sip"][:, un:un + TCH], ksscols[:, li:li + 1], None, ALU.mult)
        vv = p.esw(iv, "v")
        qkv_norm_rot(u, 1, KN, (u % 2) * TCH, depth=min(3, ATTN_LEVEL - 1))
        if ATTN_LEVEL >= 4:
            qkv_norm_rot(u, 0, QN, 0)
        for tt4 in range(4) if ATTN_LEVEL >= 3 else []:
            tt = u * 4 + tt4
            for half in range(2):
                b = 5 + half
                bk.mm_wait(b)
                for kt in range(NCT):
                    lhs = (XBU[:, kt, tt * 128:tt * 128 + 128] if kt < 4
                           else XBS[:, kt - 4, tt * 128:tt * 128 + 128])
                    ik = t.matmul(bk.t[b][:, :], lhs,
                                  wqkv[:, kt, 2 * D + half * 512:2 * D + (half + 1) * 512],
                                  start=(kt == 0), stop=(kt == NCT - 1))
                bk.wrote(b, ("t", p.inc(ik, "t")))
                bk.rd_wait(v, b)
                iv = v.tensor_copy(VB[:, tt % 8, half * 512:(half + 1) * 512], bk.t[b][:, :])
                bk.read(b, ("v", p.inc(iv, "v")))
        for qt4 in range(4) if ATTN_LEVEL >= 4 else []:
            qt = u * 4 + qt4
            r0 = max(0, qt - 2)
            nk = qt - r0 + 1
            q0 = qt4 * 128
            for h in range(H):
                pb = 64 * (h % 2)
                bk.mm_wait(0)
                first = True
                for r in range(r0, qt + 1):
                    cc0 = (r - r0) * 128
                    if r == qt - 2:
                        t.matmul(bk.t[0][:, cc0:cc0 + 128], negi[:, :], tq0[:, :],
                                 start=first, stop=False)
                        first = False
                    if r == qt:
                        t.matmul(bk.t[0][:, cc0:cc0 + 128], negi[:, :], tq2[:, :],
                                 start=first, stop=False)
                        first = False
                for r in range(r0, qt + 1):
                    cc0 = (r - r0) * 128
                    ik = t.matmul(bk.t[0][:, cc0:cc0 + 128],
                                  QN[pb:pb + 64, h // 2, q0:q0 + 128],
                                  KN[pb:pb + 64, h // 2, (r % 8) * 128:(r % 8) * 128 + 128],
                                  start=first, stop=(r == qt), tile_position=(pb, 0))
                    first = False
                bk.wrote(0, ("t", p.inc(ik, "t")))
                bk.rd_wait(v, 0)
                iv = v.reduce_max(NMC[:, 0, h:h + 1], bk.t[0][:, 0:nk * 128],
                                  axis=AX.X, negate=True)
                vv = p.esw(iv, "v")
                s.wait_ge(p.sem("v"), vv)
                ie = s.activation(bk.t[2][:, 0:nk * 128], bk.t[0][:, 0:nk * 128], AF.Exp,
                                  bias=NMC[:, 0, h:h + 1], accum_out=NMC[:, 1, h:h + 1])
                bk.read(0, ("s", p.esw(ie, "s")))
                bk.read(2, ("s", p.cnt["s"]))
                ie = s.activation(NMC[:, 1, h:h + 1], NMC[:, 1, h:h + 1], AF.Ln)
                sv = p.esw(ie, "s")
                v.wait_ge(p.sem("s"), sv)
                iv = v.tensor_sub(NMC[:, 1, h:h + 1], NMC[:, 1, h:h + 1], NMC[:, 0, h:h + 1])
                p.esw(iv, "v")
            if ATTN_LEVEL < 5:
                continue
            vv = p.cnt["v"]
            t.wait_ge(p.sem("v"), vv)
            bk.mm_wait(1)
            ik = t.transpose(bk.t[1][0:16, 0:128], NMC[:, 1, :], idn[:, :])
            bk.wrote(1, ("t", p.inc(ik, "t")))
            bk.rd_wait(v, 1)
            iv = v.tensor_copy(LSR[:, :], bk.t[1][0:16, 0:128])
            bk.read(1, ("v", p.esw(iv, "v")))
            vv = p.cnt["v"]
            t.wait_ge(p.sem("v"), vv)
            for h in range(H):
                pb = 64 * (h % 2)
                bk.mm_wait(1)
                first = True
                for r in range(r0, qt + 1):
                    cc0 = (r - r0) * 128
                    if r == qt - 2:
                        t.matmul(bk.t[1][:, cc0:cc0 + 128], negi[:, :], tt0[:, :],
                                 start=first, stop=False)
                        first = False
                    if r == qt:
                        t.matmul(bk.t[1][:, cc0:cc0 + 128], negi[:, :], tt2[:, :],
                                 start=first, stop=False)
                        first = False
                for r in range(r0, qt + 1):
                    cc0 = (r - r0) * 128
                    t.matmul(bk.t[1][:, cc0:cc0 + 128],
                             KN[pb:pb + 64, h // 2, (r % 8) * 128:(r % 8) * 128 + 128],
                             QN[pb:pb + 64, h // 2, q0:q0 + 128],
                             start=first, stop=False, tile_position=(pb, 0))
                    first = False
                    ik = t.matmul(bk.t[1][:, cc0:cc0 + 128], negsel[:, h, :],
                                  LSR[:, :], start=False, stop=(r == qt))
                bk.wrote(1, ("t", p.inc(ik, "t")))
                bk.rd_wait(s, 1)
                ie = s.activation(PRT[:, 0:nk * 128], bk.t[1][:, 0:nk * 128], AF.Exp)
                bk.read(1, ("s", p.inc(ie, "s")))
                sv = p.cnt["s"]
                t.wait_ge(p.sem("s"), sv)
                if h % 2 == 0:
                    bk.mm_wait(3)
                for r in range(r0, qt + 1):
                    ik = t.matmul(bk.t[3][pb:pb + 64, q0:q0 + 128],
                                  VB[:, r % 8, h * 64:h * 64 + 64],
                                  PRT[:, (r - r0) * 128:(r - r0 + 1) * 128],
                                  start=(r == r0), stop=(r == qt),
                                  tile_position=(0, pb))
                bk.wrote(3, ("t", p.inc(ik, "t")))
                if h % 2 == 1:
                    bk.rd_wait(v, 3)
                    iv = v.tensor_copy(AVO[:, h // 2, q0:q0 + 128], bk.t[3][:, q0:q0 + 128])
                    bk.read(3, ("v", p.inc(iv, "v")))
        vv = p.cnt["v"]
        t.wait_ge(p.sem("v"), vv)
        for mt in range(NCT) if ATTN_LEVEL >= 6 else []:
            sl = mt % 2
            if p.cnt.get("qf", 0) > 0:
                sy.wait_ge(p.sem("qf"), p.cnt["qf"])
            il = sy.dma_start(XLC[:, sl, :], xloc_fm[:, mt, n0:n0 + TCH])
            xl_tok = (f"qx{sl}", p.inc(il, f"qx{sl}", 16))
            b = 5 + mt % 2
            bk.mm_wait(b)
            for kt in range(NCT):
                ik = t.matmul(bk.t[b][:, :], wo[:, kt, mt * 128:(mt + 1) * 128],
                              AVO[:, kt, :], start=(kt == 0), stop=(kt == NCT - 1))
            bk.wrote(b, ("t", p.inc(ik, "t")))
            bk.rd_wait(v, b)
            v.wait_ge(p.sem(xl_tok[0]), xl_tok[1])
            iv = v.tensor_add(XLC[:, sl, :], XLC[:, sl, :], bk.t[b][:, :])
            bk.read(b, ("v", p.inc(iv, "v")))
            vv = p.cnt["v"]
            g.wait_ge(p.sem("v"), vv)
            ig = g.dma_start(xloc_fm[:, mt, n0:n0 + TCH], XLC[:, sl, :])
            p.inc(ig, "qf", 16)
        g.wait_ge(p.sem("qf"), p.cnt["qf"])
    p.barrier("phb2")
    sc.close()
    if "ffn" not in parts:
        return ("v", p.cnt["v"])
    return _ffn_only(p, bk, li, XLOC, w1_h, w2_h, lngb, b1c, b2c)


def _ffn_only(p, bk, li, XLOC, w1_h, w2_h, lngb, b1c, b2c):
    t, v, s, g, sy = p.engines()
    xloc_fm = XLOC.rearrange("(ct pp) t -> pp ct t", pp=128)
    sc2 = contextlib.ExitStack()
    XC = p.sb(f"g{li}xc", [128, NCT, TCH + 1], F32, sc2)
    XN2 = p.sb(f"g{li}xn2", [128, NCT, TCH], BF16, sc2)
    TB1 = p.sb(f"g{li}tb1", [128, NCT, TCH], BF16, sc2)
    TB2 = p.sb(f"g{li}tb2", [128, NCT, TCH], BF16, sc2)
    AB = p.sb(f"g{li}ab", [128, NA, TCH], BF16, sc2)
    GB = p.sb(f"g{li}gb", [128, NA, TCH], BF16, sc2)
    OUTB = p.sb(f"g{li}out", [128, NCT, TCH], F32, sc2)
    lrow = p.sb(f"g{li}lr", [1, 4, TCH], F32, sc2)
    lrow_b = p.sb(f"g{li}lrb", [1, 2, TCH], BF16, sc2)
    w1sl = [p.sb(f"g{li}w1{i}", [128, NCT, 512], BF16, sc2) for i in range(2)]
    w2f, w2_tok = load_w(p, w2_h, f"g{li}w2", 2816, D, scope=sc2)
    onesb = p.C["ones_col_b"]

    def load_w1(mg, i):
        qn = f"qw{i}"
        g.wait_ge(p.sem("t"), p.cnt["t"])
        ii = g.dma_start(w1sl[i][:, :, :],
                         w1_h[:, mg * 512:(mg + 1) * 512].rearrange("(kt pp) o -> pp kt o", pp=128))
        return (qn, p.inc(ii, qn, 16))

    for c in range(NCH):
        n0 = c * TCH
        w1tok = [load_w1(0, 0), load_w1(1, 1)]
        if c > 0:
            sy.wait_ge(p.sem("v"), p.cnt["v"])
            sy.wait_ge(p.sem("s"), p.cnt["s"])
        sy.wait_ge(p.sem("qf"), p.cnt.get("qf", 0))
        if c == 0:
            iv = v.memset(XC[:, :, 0:1], 0.0)
            p.esw(iv, "v")
            il = sy.dma_start(XC[:, :, 1:TCH + 1], xloc_fm[:, :, 0:TCH])
        else:
            il = sy.dma_start(XC[:, :, :], xloc_fm[:, :, n0 - 1:n0 + TCH])
        xc_tok = ("qx", p.inc(il, "qx", 16))
        for e in (t, v, s):
            e.wait_ge(p.sem(xc_tok[0]), xc_tok[1])
        if c == 0:
            t.wait_ge(p.sem("v"), p.cnt["v"])
            s.wait_ge(p.sem("v"), p.cnt["v"])

        def xs2(ct):
            off = 1 if ct < 4 else 0
            return XC[:, ct, off:off + TCH]

        for ct in range(NCT):
            iv = v.tensor_copy(TB1[:, ct, :], xs2(ct))
            ie = s.activation(TB2[:, ct, :], xs2(ct), AF.Square)
        vv = p.inc(iv, "v"); sv = p.inc(ie, "s")
        t.wait_ge(p.sem("v"), vv); t.wait_ge(p.sem("s"), sv)
        bk.mm_wait(0); bk.mm_wait(1)
        for ct in range(NCT):
            t.matmul(bk.t[0][0:1, :], onesb[:, 0:1], TB1[:, ct, :], start=(ct == 0), stop=(ct == NCT - 1))
        for ct in range(NCT):
            ik = t.matmul(bk.t[1][0:1, :], onesb[:, 0:1], TB2[:, ct, :], start=(ct == 0), stop=(ct == NCT - 1))
        tv = p.inc(ik, "t")
        bk.wrote(0, ("t", tv)); bk.wrote(1, ("t", tv))
        bk.rd_wait(v, 0); bk.rd_wait(v, 1)
        v.tensor_scalar(lrow[:, 0, :], bk.t[0][0:1, :], 1.0 / D, None, ALU.mult)
        iv = v.tensor_scalar(lrow[:, 1, :], bk.t[1][0:1, :], 1.0 / D, None, ALU.mult)
        bk.read(0, ("v", p.esw(iv, "v"))); bk.read(1, ("v", p.cnt["v"]))
        iv = v.tensor_mul(lrow[:, 2, :], lrow[:, 0, :], lrow[:, 0, :])
        p.esw(iv, "v")
        iv = v.tensor_sub(lrow[:, 3, :], lrow[:, 1, :], lrow[:, 2, :])
        vv = p.inc(iv, "v")
        s.wait_ge(p.sem("v"), vv)
        ie = s.activation(lrow[:, 3, :], lrow[:, 3, :], AF.Ln, bias=p.cc(1e-5, 1))
        p.esw(ie, "s")
        ie = s.activation(lrow[:, 3, :], lrow[:, 3, :], AF.Exp, scale=p.cc(-0.5, 1))
        sv = p.inc(ie, "s")
        v.wait_ge(p.sem("s"), sv)
        v.tensor_copy(lrow_b[:, 0, :], lrow[:, 0, :])
        iv = v.tensor_copy(lrow_b[:, 1, :], lrow[:, 3, :])
        vv = p.inc(iv, "v")
        t.wait_ge(p.sem("v"), vv)
        bk.mm_wait(0); bk.mm_wait(1)
        t.matmul(bk.t[0][:, :], p.C["ones_row_b"][:, :], lrow_b[:, 0, :], start=True, stop=True)
        ik = t.matmul(bk.t[1][:, :], p.C["ones_row_b"][:, :], lrow_b[:, 1, :], start=True, stop=True)
        tv = p.inc(ik, "t")
        bk.wrote(0, ("t", tv)); bk.wrote(1, ("t", tv))
        bk.rd_wait(v, 0); bk.rd_wait(v, 1)
        for ct in range(NCT):
            iv = v.tensor_sub(TB1[:, ct, :], xs2(ct), bk.t[0][:, :])
        p.esw(iv, "v")
        bk.read(0, ("v", p.cnt["v"]))
        for ct in range(NCT):
            iv = v.tensor_mul(TB2[:, ct, :], TB1[:, ct, :], bk.t[1][:, :])
        p.esw(iv, "v")
        bk.read(1, ("v", p.cnt["v"]))
        for ct in range(NCT):
            iv = v.tensor_scalar(XN2[:, ct, :], TB2[:, ct, :], lngb[:, 0, ct, li:li + 1],
                                 lngb[:, 1, ct, li:li + 1], ALU.mult, ALU.add)
        vv = p.inc(iv, "v")
        t.wait_ge(p.sem("v"), vv)
        for mg in range(11):
            i_ = mg % 2
            t.wait_ge(p.sem(w1tok[i_][0]), w1tok[i_][1])
            for m4 in range(4):
                m = mg * 4 + m4
                b = 2 + m4 % 2
                bk.mm_wait(b)
                for kt in range(NCT):
                    ik = t.matmul(bk.t[b][:, :], w1sl[i_][:, kt, m4 * 128:(m4 + 1) * 128],
                                  XN2[:, kt, :], start=(kt == 0), stop=(kt == NCT - 1))
                bk.wrote(b, ("t", p.inc(ik, "t")))
                if m < NA:
                    bk.rd_wait(v, b)
                    iv = v.tensor_scalar(AB[:, m, :], bk.t[b][:, :], b1c[:, m, li:li + 1],
                                         None, ALU.add)
                    bk.read(b, ("v", p.inc(iv, "v")))
                else:
                    bk.rd_wait(s, b)
                    ie = s.activation(GB[:, m - NA, :], bk.t[b][:, :], AF.Gelu,
                                      bias=b1c[:, m, li:li + 1])
                    bk.read(b, ("s", p.inc(ie, "s")))
            if mg + 2 < 11:
                w1tok[i_] = load_w1(mg + 2, i_)
        vv = p.cnt["v"]; sv = p.cnt["s"]
        v.wait_ge(p.sem("s"), sv)
        v.wait_ge(p.sem("v"), vv)
        for m in range(NA):
            iv = v.tensor_mul(AB[:, m, :], AB[:, m, :], GB[:, m, :])
        vv = p.esw(iv, "v")
        t.wait_ge(p.sem("v"), vv)
        t.wait_ge(p.sem("qg"), w2_tok[1])
        for mt in range(NCT):
            b = 4 + mt % 2
            bk.mm_wait(b)
            for kt in range(NA):
                ik = t.matmul(bk.t[b][:, :], w2f[:, kt, mt * 128:(mt + 1) * 128],
                              AB[:, kt, :], start=(kt == 0), stop=(kt == NA - 1))
            bk.wrote(b, ("t", p.inc(ik, "t")))
            bk.rd_wait(v, b)
            iv = v.scalar_tensor_tensor(OUTB[:, mt, :], bk.t[b][:, :], b2c[:, mt, li:li + 1],
                                        XC[:, mt, 1:TCH + 1], ALU.add, ALU.add)
            bk.read(b, ("v", p.inc(iv, "v")))
        vv = p.cnt["v"]
        g.wait_ge(p.sem("v"), vv)
        ig = g.dma_start(xloc_fm[:, :, n0:n0 + TCH], OUTB[:, :, :])
        p.inc(ig, "qf", 16)
        g.wait_ge(p.sem("qf"), p.cnt["qf"])
    p.barrier("phb3")
    sc2.close()
    return ("v", p.cnt["v"])



def _host_prep(inp):
    H_ = {}
    half = np.arange(0, 64, 2, dtype=np.float64)
    inv_freq = 1.0 / 10000.0 ** (half / 64)
    tt_ = np.arange(T, dtype=np.float64)
    freqs = np.concatenate([tt_[:, None] * inv_freq[None, :]] * 2, -1)
    base_scale = (half + 0.4 * 64) / (1.4 * 64)
    power = (tt_ - T // 2) / 128
    scl = np.concatenate([base_scale[None, :] ** power[:, None]] * 2, -1)
    cosf, sinf = np.cos(freqs), np.sin(freqs)
    sgn = np.where(np.arange(64) < 32, -1.0, 1.0)
    H_["cs"] = np.tile((cosf * scl).T, (2, 1)).astype(np.float32)
    H_["snp"] = np.tile((sinf * scl).T * sgn[:, None], (2, 1)).astype(np.float32)
    H_["ci"] = np.tile((cosf / scl).T, (2, 1)).astype(np.float32)
    H_["sip"] = np.tile((sinf / scl).T * sgn[:, None], (2, 1)).astype(np.float32)
    QS, KS = inp["L1_ffn_q_scale"], inp["L1_ffn_k_scale"]
    si = (np.arange(64) + 32) % 64
    H_["qsc"] = np.tile(QS.T, (2, 1)).astype(np.float32)
    H_["ksc"] = np.tile(KS.T, (2, 1)).astype(np.float32)
    H_["qssc"] = np.tile(QS[:, si].T, (2, 1)).astype(np.float32)
    H_["kssc"] = np.tile(KS[:, si].T, (2, 1)).astype(np.float32)
    W1, W2, B1, B2 = (inp["L1_ffn_W1"], inp["L1_ffn_W2"], inp["L1_ffn_b1"], inp["L1_ffn_b2"])
    W1p = np.zeros((6, 1024, 5632), np.float32)
    W1p[:, :, 0:2730] = W1[:, :, 0:2730]; W1p[:, :, 2816:2816 + 2730] = W1[:, :, 2730:]
    W2p = np.zeros((6, 2816, 1024), np.float32)
    W2p[:, 0:2730, :] = W2
    b1p = np.zeros((6, 5632), np.float32)
    b1p[:, 0:2730] = B1[:, 0:2730]; b1p[:, 2816:2816 + 2730] = B1[:, 2730:]
    H_["w1p"] = W1p; H_["w2p"] = W2p
    H_["b1c"] = np.ascontiguousarray(b1p.reshape(6, 44, 128).transpose(2, 1, 0))
    H_["b2c"] = np.ascontiguousarray(B2.reshape(6, 8, 128).transpose(2, 1, 0))
    H_["lngb"] = np.ascontiguousarray(np.stack([inp["L1_ffn_ln_g"], inp["L1_ffn_ln_b"]], 0)
                                      .reshape(2, 6, 8, 128).transpose(3, 0, 2, 1))
    H_["wl1"] = np.ascontiguousarray(inp["L1_attn_W"].reshape(1024, 4, 8, 128).transpose(0, 2, 1, 3))
    H_["bc4"] = np.ascontiguousarray(inp["L1_attn_b"].reshape(4, 8, 128).transpose(2, 0, 1))
    H_["inb"] = np.ascontiguousarray(inp["L1_attn_input_bias"].reshape(8, 128).T)
    H_["sclc"] = np.ascontiguousarray(inp["L1_attn_scale"].reshape(8, 128).T)
    H_["gbc"] = np.stack([inp["L0_attn_gamma"].reshape(8, 128).T,
                          inp["L0_attn_beta"].reshape(8, 128).T], 1)
    return H_


def _launch1(inp, xs):
    p = Prog()
    x_h = p.din("x", xs, percore=True)
    wq_h = p.din("wq", inp["L0_attn_W_Q"]); wk_h = p.din("wk", inp["L0_attn_W_K"])
    wv_h = p.din("wv", inp["L0_attn_W_V"]); wfc_h = p.din("wfc", inp["L0_attn_W_fc"])
    gb_h = p.din("gbcols", _host_prep.H["gbc"])
    rt_h = p.din("rt", inp["L0_ffn_router"])
    weg_h = p.din("weg", inp["L0_ffn_experts_gate"]); weu_h = p.din("weu", inp["L0_ffn_experts_up"])
    wed_h = p.din("wed", inp["L0_ffn_experts_down"])
    wsg_h = p.din("wsg", inp["L0_ffn_shared_gate"]); wsu_h = p.din("wsu", inp["L0_ffn_shared_up"])
    wsd_h = p.din("wsd", inp["L0_ffn_shared_down"])
    y_h = p.dout("xout", [128, NCT, T], F32)
    load_consts(p, ["cc", "ones_col_b", "ones_row_b", "ones_row_f", "ones_col_f",
                    "identity", "hpsel", "bc16"])
    bkp = Banks(p)
    X = p.sb("X", [128, NCT, T], F32)
    XN = p.sb("XN", [128, NCT, T], BF16)
    gb = p.sb("gb", [128, 2, NCT], F32)
    t, v, s, g, sy = p.engines()
    emit_load_x(p, x_h, X)
    p.inc(sy.dma_start(gb[...], gb_h), "qs", 16)
    for e in (t, v, s):
        e.wait_ge(p.sem("qs"), p.cnt["qs"])
    vtok = emit_ln(p, bkp, X, XN)
    vtok = emit_l0_attn(p, bkp, X, XN, wq_h, wk_h, wv_h, wfc_h, gb[:, 0, :], gb[:, 1, :], vtok)
    vtok = emit_l0_ffn(p, bkp, X, XN, rt_h, weg_h, weu_h, wed_h, wsg_h, wsu_h, wsd_h, vtok)
    sy.wait_ge(p.sem(vtok[0]), vtok[1])
    sy.wait_ge(p.sem("t"), p.cnt["t"])
    st = p.inc(sy.dma_start(y_h, X[:, :, :]), "qs", 16)
    sy.wait_ge(p.sem("qs"), st)
    finish(p)
    return [r["xout"] for r in run_program(p)]


def _launch2(inp, x1s):
    H_ = _host_prep.H
    p = Prog()
    x_h = p.din("x", x1s, percore=True)
    w_h = p.din("l1w", H_["wl1"])
    bc_h = p.din("l1bc", H_["bc4"]); inb_h = p.din("l1inb", H_["inb"])
    scl_h = p.din("l1scl", H_["sclc"])
    y_h = p.dout("xout", [128, NCT, T], F32)
    load_consts(p, ["cc", "ones_col_b", "ones_row_b", "ones_col_f", "ones_row_f"])
    bkp = Banks(p)
    X = p.sb("X", [128, NCT, T], F32)
    XN = p.sb("XN", [128, NCT, T], BF16)
    bc4 = p.sb("bc4", [128, 4, NCT], F32)
    inb = p.sb("inb", [128, NCT], F32)
    scl = p.sb("scl", [128, NCT], F32)
    t, v, s, g, sy = p.engines()
    p.inc(sy.dma_start(X[...], x_h), "qs", 16)
    p.inc(sy.dma_start(bc4[...], bc_h), "qs", 16)
    p.inc(sy.dma_start(inb[...], inb_h), "qs", 16)
    p.inc(sy.dma_start(scl[...], scl_h), "qs", 16)
    for e in (t, v, s):
        e.wait_ge(p.sem("qs"), p.cnt["qs"])
    iv = v.tensor_add(bc4[:, 0, :], bc4[:, 0, :], inb[:, :])
    p.esw(iv, "v")
    iv = v.tensor_scalar(bc4[:, 0, :], bc4[:, 0, :], 0.1, None, ALU.mult)
    p.esw(iv, "v")
    vtok = emit_ln(p, bkp, X, XN)
    vtok = emit_l1_attn(p, bkp, X, XN, w_h, bc4, scl, vtok)
    sy.wait_ge(p.sem(vtok[0]), vtok[1])
    sy.wait_ge(p.sem("t"), p.cnt["t"])
    st = p.inc(sy.dma_start(y_h, X[:, :, :]), "qs", 16)
    sy.wait_ge(p.sem("qs"), st)
    finish(p)
    res = run_program(p)
    return [r["xout"] for r in res]


def _launch3(inp, x2s, x1s):
    H_ = _host_prep.H
    p = Prog()
    x2_h = p.din("x2", x2s, percore=True)
    x1_h = p.din("x1", x1s, percore=True)
    wqkv_h = p.din("wqkv", inp["L1_ffn_Wqkv"]); wo_h = p.din("wo6", inp["L1_ffn_Wo"])
    w1_h = p.din("w1p", H_["w1p"]); w2_h = p.din("w2p", H_["w2p"])
    tb_h = {k: p.din("tab_" + k, H_[k]) for k in ["cs", "snp", "ci", "sip"]}
    qs_h = p.din("qsc", H_["qsc"]); ks_h = p.din("ksc", H_["ksc"])
    qss_h = p.din("qssc", H_["qssc"]); kss_h = p.din("kssc", H_["kssc"])
    lngb_h = p.din("lngbc", H_["lngb"]); b1c_h = p.din("b1cc", H_["b1c"])
    b2c_h = p.din("b2cc", H_["b2c"])
    y_h = p.dout("y", [128, NCT, T], F32)
    load_consts(p, ["cc", "ones_col_b", "ones_row_b", "ones_col_f", "ones_row_f",
                    "identity", "hpsel", "bc16", "psig", "negsel", "negi",
                    "tri_q0", "tri_q2", "tri_t0", "tri_t2"])
    bkp = Banks(p)
    t, v, s, g, sy = p.engines()
    tabs = {}
    for k in ["cs", "snp", "ci", "sip"]:
        tabs[k] = p.sb("tb_" + k, [128, T], BF16)
        p.inc(g.dma_start(tabs[k][...], tb_h[k]), "qg", 16)
    cols = {}
    for k, h in [("qs", qs_h), ("ks", ks_h), ("qss", qss_h), ("kss", kss_h)]:
        cols[k] = p.sb("cl_" + k, [128, 6], F32)
        p.inc(sy.dma_start(cols[k][...], h), "qs", 16)
    lngb_s = p.sb("lngbs", [128, 2, 8, 6], F32)
    p.inc(sy.dma_start(lngb_s[...], lngb_h), "qs", 16)
    b1c_s = p.sb("b1cs", [128, 44, 6], F32)
    p.inc(sy.dma_start(b1c_s[...], b1c_h), "qs", 16)
    b2c_s = p.sb("b2cs", [128, 8, 6], F32)
    p.inc(sy.dma_start(b2c_s[...], b2c_h), "qs", 16)
    XLOC = p.dram("xloc", [D, T], F32)
    scL = contextlib.ExitStack()
    X1 = p.sb("X1", [128, NCT, T], F32, scL)
    XNB = p.sb("XNB", [128, NCT, T], BF16, scL)
    p.inc(sy.dma_start(X1[...], x1_h), "qs", 16)
    for e in (t, v, s):
        e.wait_ge(p.sem("qs"), p.cnt["qs"])
    vtok = emit_ln(p, bkp, X1, XNB, xn_f32=X1)
    g.wait_ge(p.sem(vtok[0]), vtok[1])
    ii = g.dma_start(XLOC.rearrange("(ct pp) t -> pp ct t", pp=128), X1[:, :, :])
    p.inc(ii, "qf", 16)
    g.wait_ge(p.sem("qf"), p.cnt["qf"])
    g.wait_ge(p.sem("qg"), p.cnt["qg"])
    p.barrier("lnfree")
    scL.close()
    for li in range(6):
        emit_l1_ffn_layer(p, bkp, li, XLOC, wqkv_h[li], wo_h[li], w1_h[li], w2_h[li], tabs,
                          cols["qs"], cols["ks"], cols["qss"], cols["kss"],
                          lngb_s, b1c_s, b2c_s)
    p.barrier("fin")
    FA = p.sb("finA", [128, NCT, TCH], F32)
    FB = p.sb("finB", [128, NCT, TCH], F32)
    xloc_fm = XLOC.rearrange("(ct pp) t -> pp ct t", pp=128)
    for c in range(NCH):
        n0 = c * TCH
        if c > 0:
            sy.wait_ge(p.sem("v"), p.cnt["v"])
            sy.wait_ge(p.sem("qy"), p.cnt["qy"])
        il = sy.dma_start(FA[:, :, :], x2_h[:, :, n0:n0 + TCH])
        p.inc(il, "qs", 16)
        il = sy.dma_start(FB[:, :, :], xloc_fm[:, :, n0:n0 + TCH])
        tokf = p.inc(il, "qs", 16)
        v.wait_ge(p.sem("qs"), tokf)
        for ct in range(NCT):
            iv = v.tensor_add(FA[:, ct, :], FA[:, ct, :], FB[:, ct, :])
        vv = p.esw(iv, "v")
        g.wait_ge(p.sem("v"), vv)
        ig = g.dma_start(y_h[:, :, n0:n0 + TCH], FA[:, :, :])
        p.inc(ig, "qy", 16)
        g.wait_ge(p.sem("qy"), p.cnt["qy"])
    finish(p)
    return [r["y"] for r in run_program(p)]


def kernel(**inputs):
    inp = inputs
    _host_prep.H = _host_prep(inp)
    x = inp["L0_attn_inputs"]
    xs = [np.ascontiguousarray(x[b].T) for b in range(8)]
    x1 = _launch1(inp, xs)
    x1 = [np.ascontiguousarray(a) for a in x1]
    x2 = _launch2(inp, x1)
    y = _launch3(inp, [np.ascontiguousarray(a) for a in x2],
                 [np.ascontiguousarray(a) for a in x1])
    out = np.empty((8, T, D), np.float32)
    for b in range(8):
        out[b] = np.asarray(y[b]).transpose(1, 0, 2).reshape(D, T).T
    return out


# version 08 (final)
# speedup vs baseline: 1.0122x; 1.0122x over previous
import contextlib
import numpy as np
import ml_dtypes
import concourse.bass as bass
import concourse.mybir as mybir
from concourse.bass import broadcast_tensor_aps
from concourse.bass_utils import run_bass_kernel_spmd

F32 = mybir.dt.float32
BF16 = mybir.dt.bfloat16
AF = mybir.ActivationFunctionType
ALU = mybir.AluOpType
AX = mybir.AxisListType
BF16NP = ml_dtypes.bfloat16

T = 2048
D = 1024
NCT = 8
NTT = 16
H = 16
DH = 64
TCH = 512
NCH = T // TCH

CC_VALS = [1e-5, -0.5, 1.0, 1e-8, 1e-6, 1e-12, 0.1, 10.0, -1.0, 0.5, 8.0, -0.125, 0.125]
CCI = {v: i for i, v in enumerate(CC_VALS)}


class Prog:
    def __init__(self):
        self.nc = bass.Bass("TRN2", target_bir_lowering=False)
        self.es = contextlib.ExitStack()
        self._sems = {}
        self.cnt = {}
        self.host = {}
        self.host_percore = {}
        self.out_names = []
        self.C = {}
        self.uid = 0

    def sem(self, name):
        if name not in self._sems:
            self._sems[name] = self.es.enter_context(self.nc.semaphore(name))
            self.cnt[name] = 0
        return self._sems[name]

    def inc(self, inst, name, k=1):
        inst.then_inc(self.sem(name), k)
        self.cnt[name] += k
        return self.cnt[name]

    def mark(self, eng, name):
        return self.inc(eng.nop(), name, 1)

    def wait(self, eng, name, v=None):
        eng.wait_ge(self.sem(name), self.cnt[name] if v is None else v)

    def esw(self, inst, name):
        eng = {"v": self.nc.vector, "s": self.nc.scalar, "g": self.nc.gpsimd}[name]
        vv = self.inc(inst, name)
        eng.wait_ge(self.sem(name), vv)
        return vv

    def din(self, name, arr, dtype=None, percore=False):
        a0 = arr[0] if percore else arr
        dt = dtype or (F32 if a0.dtype == np.float32 else BF16)
        h = self.nc.dram_tensor(name, list(a0.shape), dt, kind="ExternalInput").ap()
        if percore:
            self.host_percore[name] = arr
        else:
            self.host[name] = arr
        return h

    def dout(self, name, shape, dtype=F32):
        self.out_names.append(name)
        return self.nc.dram_tensor(name, list(shape), dtype, kind="ExternalOutput").ap()

    def dram(self, name, shape, dtype=F32):
        return self.nc.dram_tensor(name, list(shape), dtype, kind="Internal").ap()

    def sb(self, name, shape, dtype=F32, scope=None):
        cm = self.nc.sbuf_tensor(name, list(shape), dtype)
        return (scope or self.es).enter_context(cm)

    def ps(self, name, shape, dtype=F32, scope=None):
        cm = self.nc.psum_tensor(name, list(shape), dtype)
        return (scope or self.es).enter_context(cm)

    def engines(self):
        nc = self.nc
        return nc.tensor, nc.vector, nc.scalar, nc.gpsimd, nc.sync

    def barrier(self, tag="bar"):
        t, v, s, g, sy = self.engines()
        for e, nm in ((t, "t"), (v, "v"), (s, "s"), (g, "g")):
            if self.cnt.get(nm, 0) > 0:
                e.wait_ge(self.sem(nm), self.cnt[nm])
        engs = [t, v, s, g, sy]
        sm = self.sem(tag)
        for e in engs:
            self.inc(e.nop(), tag, 1)
        tot = self.cnt[tag]
        for e in engs:
            e.wait_ge(sm, tot)

    def cc(self, val, parts=128):
        return self.C["cc"][0:parts, CCI[val]:CCI[val] + 1]



def host_consts():
    C = {}
    C["cc"] = np.tile(np.array([CC_VALS], np.float32), (128, 1))
    C["ones_col_b"] = np.ones((128, 1), BF16NP)
    C["ones_row_b"] = np.ones((1, 128), BF16NP)
    C["ones_row_f"] = np.ones((1, 128), np.float32)
    C["ones_col_f"] = np.ones((128, 1), np.float32)
    C["identity"] = np.eye(128, dtype=np.float32)
    hp = np.zeros((8, 128, 16), np.float32)
    for ct in range(8):
        hp[ct, 0:64, 2 * ct] = 1
        hp[ct, 64:128, 2 * ct + 1] = 1
    C["hpsel"] = np.ascontiguousarray(hp.transpose(1, 0, 2)).astype(BF16NP)
    bc = np.zeros((8, 16, 128), np.float32)
    for ct in range(8):
        bc[ct, 2 * ct, 0:64] = 1
        bc[ct, 2 * ct + 1, 64:128] = 1
    C["bc16"] = np.ascontiguousarray(bc.transpose(1, 0, 2)).astype(BF16NP)
    sig = np.zeros((128, 128), np.float32)
    for m in range(128):
        sm = (m // 64) * 64 + ((m % 64 + 32) % 64)
        sig[sm, m] = 1
    C["psig"] = sig.astype(BF16NP)
    ql, kl = np.meshgrid(np.arange(128), np.arange(128), indexing="ij")
    C["tri_q0"] = (kl < ql).astype(np.float32).astype(BF16NP)
    C["tri_q2"] = (kl > ql).astype(np.float32).astype(BF16NP)
    C["tri_t0"] = np.ascontiguousarray(C["tri_q0"].T)
    C["tri_t2"] = np.ascontiguousarray(C["tri_q2"].T)
    C["negi"] = (np.eye(128) * -1e9).astype(BF16NP)
    ns = np.zeros((16, 16, 128), np.float32)
    for hh in range(16):
        ns[hh, hh, :] = -1.0
    C["negsel"] = ns
    return C


def load_consts(p, names=None):
    hc = host_consts()
    for name, arr in hc.items():
        if names is not None and name not in names:
            continue
        h = p.din("c_" + name, arr)
        sbt = p.sb("C" + name, arr.shape, F32 if arr.dtype == np.float32 else BF16)
        p.inc(p.nc.sync.dma_start(sbt[...], h), "qs", 16)
        p.C[name] = sbt



class Banks:
    def __init__(self, p, n=8, name="pb"):
        self.p = p
        self.t = [p.ps(f"{name}{i}", [128, 512]) for i in range(n)]
        self.rd = [[] for _ in range(n)]
        self.wr = [None] * n

    def mm_wait(self, i):
        for tok in self.rd[i]:
            self.p.nc.tensor.wait_ge(self.p.sem(tok[0]), tok[1])
        self.rd[i] = []

    def wrote(self, i, tok):
        self.wr[i] = tok

    def rd_wait(self, eng, i):
        if self.wr[i] is not None:
            eng.wait_ge(self.p.sem(self.wr[i][0]), self.wr[i][1])

    def read(self, i, tok):
        self.rd[i].append(tok)



def emit_ln(p, bk, src, xn_out, xn_f32=None, scale_cols=None, bias_cols=None,
            src_tok=None, eps=1e-5):
    t, v, s, g, sy = p.engines()
    p.barrier("phb")
    u = p.uid; p.uid += 1
    sc = contextlib.ExitStack()
    tmpb = p.sb(f"lnb{u}", [128, NCT, TCH], BF16, sc)
    tmpq = p.sb(f"lnq{u}", [128, NCT, TCH], BF16, sc)
    rows = p.sb(f"lnr{u}", [1, 4, TCH], F32, sc)
    rows_b = p.sb(f"lnrb{u}", [1, 2, TCH], BF16, sc)
    ps_s, ps_q = bk.t[0][0:1, :], bk.t[1][0:1, :]
    ps_mu, ps_rs = bk.t[2], bk.t[3]
    ones_b = p.C["ones_col_b"]
    onesr = p.C["ones_row_b"]
    eps_c = p.cc(eps, 1)
    for c in range(NCH):
        n0 = c * TCH
        if src_tok is not None:
            v.wait_ge(p.sem(src_tok[0]), src_tok[1])
            s.wait_ge(p.sem(src_tok[0]), src_tok[1])
        if c > 0:
            v.wait_ge(p.sem("v"), p.cnt["v"])
            v.wait_ge(p.sem("t"), p.cnt["t"])
            s.wait_ge(p.sem("t"), p.cnt["t"])
        for ct in range(NCT):
            iv = v.tensor_copy(tmpb[:, ct, :], src[:, ct, n0:n0 + TCH])
            isq = s.activation(tmpq[:, ct, :], src[:, ct, n0:n0 + TCH], AF.Square)
        vv = p.inc(iv, "v")
        sv = p.inc(isq, "s")
        t.wait_ge(p.sem("v"), vv)
        t.wait_ge(p.sem("s"), sv)
        bk.mm_wait(0)
        bk.mm_wait(1)
        for ct in range(NCT):
            t.matmul(ps_s, ones_b[:, 0:1], tmpb[:, ct, :], start=(ct == 0), stop=(ct == NCT - 1))
        for ct in range(NCT):
            it = t.matmul(ps_q, ones_b[:, 0:1], tmpq[:, ct, :], start=(ct == 0), stop=(ct == NCT - 1))
        tv = p.inc(it, "t")
        bk.wrote(0, ("t", tv))
        bk.wrote(1, ("t", tv))
        bk.rd_wait(v, 0)
        bk.rd_wait(v, 1)
        v.tensor_scalar(rows[:, 0, :], ps_s, 1.0 / D, None, ALU.mult)
        iv = v.tensor_scalar(rows[:, 1, :], ps_q, 1.0 / D, None, ALU.mult)
        bk.read(0, ("v", p.esw(iv, "v")))
        bk.read(1, ("v", p.cnt["v"]))
        iv = v.tensor_mul(rows[:, 2, :], rows[:, 0, :], rows[:, 0, :])
        p.esw(iv, "v")
        iv = v.tensor_sub(rows[:, 3, :], rows[:, 1, :], rows[:, 2, :])
        vv = p.inc(iv, "v")
        s.wait_ge(p.sem("v"), vv)
        isq = s.activation(rows[:, 3, :], rows[:, 3, :], AF.Ln, bias=eps_c)
        p.esw(isq, "s")
        isq = s.activation(rows[:, 3, :], rows[:, 3, :], AF.Exp, scale=p.cc(-0.5, 1))
        sv = p.inc(isq, "s")
        v.wait_ge(p.sem("s"), sv)
        v.tensor_copy(rows_b[:, 0, :], rows[:, 0, :])
        iv = v.tensor_copy(rows_b[:, 1, :], rows[:, 3, :])
        vv = p.inc(iv, "v")
        t.wait_ge(p.sem("v"), vv)
        bk.mm_wait(2)
        bk.mm_wait(3)
        t.matmul(ps_mu[:, :], onesr[:, :], rows_b[:, 0, :], start=True, stop=True)
        it = t.matmul(ps_rs[:, :], onesr[:, :], rows_b[:, 1, :], start=True, stop=True)
        tv = p.inc(it, "t")
        bk.wrote(2, ("t", tv))
        bk.wrote(3, ("t", tv))
        bk.rd_wait(v, 2)
        bk.rd_wait(v, 3)
        for ct in range(NCT):
            iv = v.tensor_sub(tmpb[:, ct, :], src[:, ct, n0:n0 + TCH], ps_mu[:, :])
        p.esw(iv, "v")
        bk.read(2, ("v", p.cnt["v"]))
        for ct in range(NCT):
            if xn_f32 is not None:
                iv = v.tensor_mul(xn_f32[:, ct, n0:n0 + TCH], tmpb[:, ct, :], ps_rs[:, :])
            if scale_cols is None:
                iv = v.tensor_mul(xn_out[:, ct, n0:n0 + TCH], tmpb[:, ct, :], ps_rs[:, :])
            else:
                iv = v.tensor_mul(tmpq[:, ct, :], tmpb[:, ct, :], ps_rs[:, :])
        if scale_cols is not None:
            p.esw(iv, "v")
            for ct in range(NCT):
                iv = v.tensor_scalar(xn_out[:, ct, n0:n0 + TCH], tmpq[:, ct, :],
                                     scale_cols[:, ct:ct + 1], bias_cols[:, ct:ct + 1],
                                     ALU.mult, ALU.add)
        vv = p.inc(iv, "v")
        bk.read(3, ("v", vv))
    sc.close()
    return ("v", p.cnt["v"])



def emit_load_x(p, x_src_hbm, X):
    i = p.nc.sync.dma_start(X[:, :, :], x_src_hbm.rearrange("(ct pp) t -> pp ct t", pp=128))
    return ("qs", p.inc(i, "qs", 16))


def emit_store_x(p, X, dst_hbm, wait_tok=None, q="qs"):
    eng = p.nc.gpsimd if q == "qg" else p.nc.sync
    if wait_tok is not None:
        eng.wait_ge(p.sem(wait_tok[0]), wait_tok[1])
    i = eng.dma_start(dst_hbm.rearrange("(ct pp) t -> pp ct t", pp=128), X[:, :, :])
    return (q, p.inc(i, q, 16))


def finish(p):
    p.es.close()


def run_program(p, n_cores=8):
    in_maps = []
    for c in range(n_cores):
        m = dict(p.host)
        for k, arrs in p.host_percore.items():
            m[k] = arrs[c]
        in_maps.append(m)
    res = run_bass_kernel_spmd(p.nc, in_maps, core_ids=list(range(n_cores)))
    return res.results


def load_w(p, hbm_ap, name, kdim, odim, q="qg", scope=None):
    sbt = p.sb(name + "_sb", [128, kdim // 128, odim], BF16, scope)
    eng = p.nc.gpsimd if q == "qg" else p.nc.sync
    i = eng.dma_start(sbt[:, :, :], hbm_ap.rearrange("(kt pp) o -> pp kt o", pp=128))
    return sbt, (q, p.inc(i, q, 16))



def emit_l0_attn(p, bk, X, XN, wq_h, wk_h, wv_h, wfc_h, gcols, bcols, xn_tok):
    t, v, s, g, sy = p.engines()
    p.barrier("phb")
    scF = contextlib.ExitStack()
    KV = p.sb("l0KV", [128, 8, 64], BF16, scF)
    ksum = p.sb("l0ks", [128, 8], F32, scF)
    sc = contextlib.ExitStack()
    Kf = p.sb("l0K", [128, NTT, D], BF16, sc)
    Vf = p.sb("l0V", [128, NTT, D], BF16, sc)
    wbuf = p.sb("l0wkv", [128, NCT, D], BF16, sc)
    i = g.dma_start(wbuf[:, :, :], wk_h.rearrange("(kt pp) o -> pp kt o", pp=128))
    wk_tok = ("qg", p.inc(i, "qg", 16))
    t.wait_ge(p.sem(xn_tok[0]), xn_tok[1])
    for part, KVf, act in ((0, Kf, AF.Sigmoid), (1, Vf, None)):
        t.wait_ge(p.sem("qg"), p.cnt["qg"])
        for tt in range(NTT):
            for half in range(2):
                b = ((tt * 2 + half) % 4) + part * 4
                bk.mm_wait(b)
                for kt in range(NCT):
                    ik = t.matmul(bk.t[b][:, :],
                                  XN[:, kt, tt * 128:(tt + 1) * 128],
                                  wbuf[:, kt, half * 512:(half + 1) * 512],
                                  start=(kt == 0), stop=(kt == NCT - 1))
                bk.wrote(b, ("t", p.inc(ik, "t")))
                if act is not None:
                    bk.rd_wait(s, b)
                    ie = s.activation(KVf[:, tt, half * 512:(half + 1) * 512],
                                      bk.t[b][:, :], act)
                    bk.read(b, ("s", p.inc(ie, "s")))
                else:
                    bk.rd_wait(v, b)
                    ie = v.tensor_copy(KVf[:, tt, half * 512:(half + 1) * 512], bk.t[b][:, :])
                    bk.read(b, ("v", p.inc(ie, "v")))
        if part == 0:
            g.wait_ge(p.sem("t"), p.cnt["t"])
            i = g.dma_start(wbuf[:, :, :], wv_h.rearrange("(kt pp) o -> pp kt o", pp=128))
            p.inc(i, "qg", 16)
    t.wait_ge(p.sem("s"), p.cnt["s"])
    t.wait_ge(p.sem("v"), p.cnt["v"])
    bk.mm_wait(0)
    bk.mm_wait(1)
    kvps = bk.t[0]
    for h in range(H):
        pb = 64 * (h % 2)
        for tt in range(NTT):
            ik = t.matmul(kvps[pb:pb + 64, (h // 2) * 64:(h // 2) * 64 + 64],
                          Kf[:, tt, h * 64:h * 64 + 64], Vf[:, tt, h * 64:h * 64 + 64],
                          start=(tt == 0), stop=(tt == NTT - 1),
                          tile_position=(0, pb))
    bk.wrote(0, ("t", p.inc(ik, "t")))
    ksps = bk.t[1]
    onesb = p.C["ones_col_b"]
    for ct in range(NCT):
        for tt in range(NTT):
            ik = t.matmul(ksps[:, ct:ct + 1], Kf[:, tt, ct * 128:(ct + 1) * 128],
                          onesb[:, 0:1], start=(tt == 0), stop=(tt == NTT - 1))
    bk.wrote(1, ("t", p.inc(ik, "t")))
    bk.rd_wait(v, 0)
    ivv = v.tensor_copy(KV[:, :, :], kvps[:, :].rearrange("p (a b) -> p a b", a=8))
    bk.read(0, ("v", p.inc(ivv, "v")))
    bk.rd_wait(v, 1)
    ivv = v.tensor_copy(ksum[:, :], ksps[:, 0:8])
    bk.read(1, ("v", p.inc(ivv, "v")))
    sc.close()

    p.barrier("l0bar")

    sc2 = contextlib.ExitStack()
    wq, wq_tok = load_w(p, wq_h, "l0wq", D, D, scope=sc2)
    wfc, wfc_tok = load_w(p, wfc_h, "l0wfc", D, D, scope=sc2)
    Qb = p.sb("l0Q", [128, NCT, TCH], BF16, sc2)
    tbA = p.sb("l0tA", [128, NCT, TCH], BF16, sc2)
    tbB = p.sb("l0tB", [128, NCT, TCH], BF16, sc2)
    X2 = p.sb("l0X2", [128, NCT, TCH], F32, sc2)
    rows = p.sb("l0rw", [16, 3, TCH], F32, sc2)
    rows_b = p.sb("l0rb", [16, TCH], BF16, sc2)
    lrows = p.sb("l0lr", [1, 4, TCH], F32, sc2)
    lrows_b = p.sb("l0lrb", [1, 2, TCH], BF16, sc2)
    hpsel = p.C["hpsel"]
    bc16 = p.C["bc16"]
    onesr = p.C["ones_row_b"]
    t.wait_ge(p.sem("qg"), wfc_tok[1])
    for c in range(NCH):
        n0 = c * TCH
        if c > 0:
            v.wait_ge(p.sem("v"), p.cnt["v"])
            s.wait_ge(p.sem("v"), p.cnt["v"])
        for ct in range(NCT):
            b = ct % 2
            bk.mm_wait(b)
            for kt in range(NCT):
                ik = t.matmul(bk.t[b][:, :], wq[:, kt, ct * 128:(ct + 1) * 128],
                              XN[:, kt, n0:n0 + TCH], start=(kt == 0), stop=(kt == NCT - 1))
            bk.wrote(b, ("t", p.inc(ik, "t")))
            bk.rd_wait(s, b)
            isg = s.activation(Qb[:, ct, :], bk.t[b][:, :], AF.Sigmoid)
            bk.read(b, ("s", p.inc(isg, "s")))
        sv = p.cnt["s"]
        v.wait_ge(p.sem("s"), sv)
        for ct in range(NCT):
            iv = v.tensor_scalar(tbA[:, ct, :], Qb[:, ct, :], ksum[:, ct:ct + 1], None, ALU.mult)
        vv = p.inc(iv, "v")
        t.wait_ge(p.sem("v"), vv)
        bk.mm_wait(2)
        for ct in range(NCT):
            ik = t.matmul(bk.t[2][0:16, :], hpsel[:, ct, :], tbA[:, ct, :],
                          start=(ct == 0), stop=(ct == NCT - 1))
        bk.wrote(2, ("t", p.inc(ik, "t")))
        bk.rd_wait(s, 2)
        isg = s.activation(rows[:, 0, :], bk.t[2][0:16, :], AF.Ln, bias=p.cc(1e-8, 16))
        bk.read(2, ("s", p.esw(isg, "s")))
        isg = s.activation(rows[:, 1, :], rows[:, 0, :], AF.Exp, scale=p.cc(-1.0, 16))
        sv = p.inc(isg, "s")
        v.wait_ge(p.sem("s"), sv)
        iv = v.tensor_copy(rows_b[:, :], rows[:, 1, :])
        vv = p.inc(iv, "v")
        t.wait_ge(p.sem("v"), vv)
        for ct in range(NCT):
            b = 3 + ct % 2
            bk.mm_wait(b)
            for hh in range(2):
                h = 2 * ct + hh
                pb = 64 * hh
                ik = t.matmul(bk.t[b][pb:pb + 64, :], KV[pb:pb + 64, ct, :],
                              Qb[pb:pb + 64, ct, :], start=True, stop=True,
                              tile_position=(pb, pb))
            bk.wrote(b, ("t", p.inc(ik, "t")))
            bk.rd_wait(s, b)
            ie = s.activation(tbA[:, ct, :], bk.t[b][:, :], AF.Identity)
            sv = p.inc(ie, "s")
            bk.read(b, ("s", sv))
            b5 = 5 if ct % 2 == 0 else 6
            bk.mm_wait(b5)
            ik = t.matmul(bk.t[b5][:, :], bc16[:, ct, :], rows_b[:, :], start=True, stop=True)
            bk.wrote(b5, ("t", p.inc(ik, "t")))
            bk.rd_wait(v, b5)
            v.wait_ge(p.sem("s"), sv)
            iv = v.tensor_mul(tbB[:, ct, :], tbA[:, ct, :], bk.t[b5][:, :])
            bk.read(b5, ("v", p.inc(iv, "v")))
        vv = p.cnt["v"]
        t.wait_ge(p.sem("v"), vv)
        for ct in range(NCT):
            b = ct % 2
            bk.mm_wait(b)
            for kt in range(NCT):
                ik = t.matmul(bk.t[b][:, :], wfc[:, kt, ct * 128:(ct + 1) * 128],
                              tbB[:, kt, :], start=(kt == 0), stop=(kt == NCT - 1))
            bk.wrote(b, ("t", p.inc(ik, "t")))
            bk.rd_wait(v, b)
            iv = v.tensor_add(X2[:, ct, :], XN[:, ct, n0:n0 + TCH], bk.t[b][:, :])
            bk.read(b, ("v", p.inc(iv, "v")))
        v.wait_ge(p.sem("v"), p.cnt["v"])
        vv = p.inc(v.nop(), "v")
        s.wait_ge(p.sem("v"), vv)
        for ct in range(NCT):
            iv = v.tensor_copy(tbA[:, ct, :], X2[:, ct, :])
            isq = s.activation(tbB[:, ct, :], X2[:, ct, :], AF.Square)
        vv = p.inc(iv, "v")
        sv = p.inc(isq, "s")
        t.wait_ge(p.sem("v"), vv)
        t.wait_ge(p.sem("s"), sv)
        bk.mm_wait(2)
        bk.mm_wait(7)
        for ct in range(NCT):
            t.matmul(bk.t[2][0:1, :], onesb[:, 0:1], tbA[:, ct, :],
                     start=(ct == 0), stop=(ct == NCT - 1))
        for ct in range(NCT):
            ik = t.matmul(bk.t[7][0:1, :], onesb[:, 0:1], tbB[:, ct, :],
                          start=(ct == 0), stop=(ct == NCT - 1))
        tv = p.inc(ik, "t")
        bk.wrote(2, ("t", tv))
        bk.wrote(7, ("t", tv))
        bk.rd_wait(v, 2)
        bk.rd_wait(v, 7)
        v.tensor_scalar(lrows[:, 0, :], bk.t[2][0:1, :], 1.0 / D, None, ALU.mult)
        iv = v.tensor_scalar(lrows[:, 1, :], bk.t[7][0:1, :], 1.0 / D, None, ALU.mult)
        bk.read(2, ("v", p.esw(iv, "v")))
        bk.read(7, ("v", p.cnt["v"]))
        iv = v.tensor_mul(lrows[:, 2, :], lrows[:, 0, :], lrows[:, 0, :])
        p.esw(iv, "v")
        iv = v.tensor_sub(lrows[:, 3, :], lrows[:, 1, :], lrows[:, 2, :])
        vv = p.inc(iv, "v")
        s.wait_ge(p.sem("v"), vv)
        isg = s.activation(lrows[:, 3, :], lrows[:, 3, :], AF.Ln, bias=p.cc(1e-5, 1))
        p.esw(isg, "s")
        isg = s.activation(lrows[:, 3, :], lrows[:, 3, :], AF.Exp, scale=p.cc(-0.5, 1))
        sv = p.inc(isg, "s")
        v.wait_ge(p.sem("s"), sv)
        v.tensor_copy(lrows_b[:, 0, :], lrows[:, 0, :])
        iv = v.tensor_copy(lrows_b[:, 1, :], lrows[:, 3, :])
        vv = p.inc(iv, "v")
        t.wait_ge(p.sem("v"), vv)
        bk.mm_wait(2)
        bk.mm_wait(7)
        t.matmul(bk.t[2][:, :], onesr[:, :], lrows_b[:, 0, :], start=True, stop=True)
        ik = t.matmul(bk.t[7][:, :], onesr[:, :], lrows_b[:, 1, :], start=True, stop=True)
        tv = p.inc(ik, "t")
        bk.wrote(2, ("t", tv))
        bk.wrote(7, ("t", tv))
        bk.rd_wait(v, 2)
        bk.rd_wait(v, 7)
        for ct in range(NCT):
            iv = v.tensor_sub(tbA[:, ct, :], X2[:, ct, :], bk.t[2][:, :])
        p.esw(iv, "v")
        bk.read(2, ("v", p.cnt["v"]))
        for ct in range(NCT):
            iv = v.tensor_mul(tbB[:, ct, :], tbA[:, ct, :], bk.t[7][:, :])
        p.esw(iv, "v")
        bk.read(7, ("v", p.cnt["v"]))
        for ct in range(NCT):
            iv = v.tensor_scalar(tbA[:, ct, :], tbB[:, ct, :], gcols[:, ct:ct + 1],
                                 bcols[:, ct:ct + 1], ALU.mult, ALU.add)
        p.esw(iv, "v")
        for ct in range(NCT):
            iv = v.tensor_add(X[:, ct, n0:n0 + TCH], X[:, ct, n0:n0 + TCH], tbA[:, ct, :])
        vv = p.inc(iv, "v")
    sc2.close()
    scF.close()
    return ("v", p.cnt["v"])



def emit_l0_ffn(p, bk, X, XN, router_h, weg_h, weu_h, wed_h, wsg_h, wsu_h, wsd_h, xn_tok,
                coeft_out=None, stop_after=None):
    t, v, s, g, sy = p.engines()
    p.barrier("phb")
    sc = contextlib.ExitStack()
    E = 8
    rtr, rtr_tok = load_w(p, router_h, "rtr", D, E, scope=sc)
    COEFT = p.sb("coeft", [128, NTT, E], F32, sc)
    COEFF = p.sb("coeff", [E, T], F32, sc)
    LGs = p.sb("lgs", [128, NTT, E], F32, sc)
    NM1 = p.sb("nm1", [128, 4, NTT], F32, sc)
    D3 = p.sb("d3", [128, NTT, E], F32, sc)
    E3 = p.sb("e3", [128, NTT, E], F32, sc)
    I3 = p.sb("i3", [128, NTT, E], F32, sc)
    t.wait_ge(p.sem(xn_tok[0]), xn_tok[1])
    t.wait_ge(p.sem("qg"), rtr_tok[1])
    for tt in range(NTT):
        b = tt % 2
        bk.mm_wait(b)
        for kt in range(NCT):
            ik = t.matmul(bk.t[b][:, 0:E], XN[:, kt, tt * 128:(tt + 1) * 128],
                          rtr[:, kt, :], start=(kt == 0), stop=(kt == NCT - 1))
        bk.wrote(b, ("t", p.inc(ik, "t")))
        bk.rd_wait(v, b)
        iv = v.tensor_copy(LGs[:, tt, :], bk.t[b][:, 0:E])
        bk.read(b, ("v", p.inc(iv, "v")))

    def vsw(inst):
        vv = p.inc(inst, "v")
        v.wait_ge(p.sem("v"), vv)

    lg3 = LGs[:, :, :]
    v.wait_ge(p.sem("v"), p.cnt["v"])
    vsw(v.reduce_max(NM1[:, 0, :], lg3, axis=AX.X, negate=True))
    a1, a2 = broadcast_tensor_aps(lg3, NM1[:, 0, :, None])
    iv = v.tensor_tensor(D3[:, :, :], a1, a2, ALU.add)
    sv = p.inc(iv, "v")
    s.wait_ge(p.sem("v"), sv)
    isg = s.activation(E3[:, :, :], D3[:, :, :], AF.Exp)
    sv = p.inc(isg, "s")
    v.wait_ge(p.sem("s"), sv)
    vsw(v.reduce_sum(NM1[:, 1, :], E3[:, :, :], axis=AX.X))
    vsw(v.reciprocal(NM1[:, 2, :], NM1[:, 1, :]))
    vsw(v.tensor_scalar(I3[:, :, :], D3[:, :, :], 0.0, None, ALU.is_ge))
    vsw(v.scalar_tensor_tensor(D3[:, :, :], I3[:, :, :], -1e9, D3[:, :, :], ALU.mult, ALU.add))
    vsw(v.reduce_max(NM1[:, 3, :], D3[:, :, :], axis=AX.X))
    a1, a2 = broadcast_tensor_aps(D3[:, :, :], NM1[:, 3, :, None])
    vsw(v.tensor_tensor(D3[:, :, :], a1, a2, ALU.is_ge))
    vsw(v.tensor_add(I3[:, :, :], I3[:, :, :], D3[:, :, :]))
    vsw(v.tensor_mul(I3[:, :, :], I3[:, :, :], E3[:, :, :]))
    a1, a2 = broadcast_tensor_aps(I3[:, :, :], NM1[:, 2, :, None])
    vsw(v.tensor_tensor(COEFT[:, :, :], a1, a2, ALU.mult))
    t.wait_ge(p.sem("v"), p.cnt["v"])
    idn = p.C["identity"]
    for bb in range(4):
        bk.mm_wait(2 + bb)
    for tt in range(NTT):
        ik = t.transpose(bk.t[2 + tt // 4][0:E, (tt % 4) * 128:(tt % 4) * 128 + 128],
                         COEFT[:, tt, :], idn[:, :])
    tv = p.inc(ik, "t")
    for bb in range(4):
        bk.wrote(2 + bb, ("t", tv))
        bk.rd_wait(v, 2 + bb)
        iv = v.tensor_copy(COEFF[:, bb * 512:(bb + 1) * 512], bk.t[2 + bb][0:E, 0:512])
        bk.read(2 + bb, ("v", p.inc(iv, "v")))
    if stop_after == "coef":
        if coeft_out is not None:
            iv = v.tensor_copy(coeft_out[:, 0:16, :], COEFT[:, :, :])
            iv = v.tensor_copy(coeft_out[:, 16:32, :], LGs[:, :, :])
            iv = v.tensor_copy(coeft_out[:, 32:40, :], NM1[:, 0:4, :].rearrange("p a b -> p (a b)").rearrange("p (a e) -> p a e", e=8))
            iv = v.tensor_copy(coeft_out[:, 40:48, :], rtr[:, :, :])
            p.inc(iv, "v")
        sc.close()
        return ("v", p.cnt["v"])
    onesf = p.C["ones_row_f"]
    crow = p.sb("crow", [1, 2, T], F32, sc)
    cscr = p.dram(f"coefscr{p.uid}", [E, T], F32)
    p.uid += 1
    g.wait_ge(p.sem("v"), p.cnt["v"])
    i = g.dma_start(cscr, COEFF[:, :])
    p.inc(i, "qg", 16)

    def load_crow(e, i2):
        qn = f"qe{i2}"
        ii = g.dma_start(crow[:, i2, :], cscr[e:e + 1, :])
        return (qn, p.inc(ii, qn, 16))

    Gb = p.sb("moeg", [128, 8, TCH], BF16, sc)
    Ub = p.sb("moeu", [128, 8, TCH], BF16, sc)
    Hb = p.sb("moeh", [128, 8, TCH], BF16, sc)

    def path(wg, wu, wd, dt, coef_row):
        for c in range(NCH):
            n0 = c * TCH
            s.wait_ge(p.sem("v"), p.mul_tok) if False else None
            s.wait_ge(p.sem("v"), p.cnt["v"])
            v.wait_ge(p.sem("t"), p.cnt["t"])
            for mt in range(dt):
                b = mt % 2
                bk.mm_wait(b)
                for kt in range(NCT):
                    ik = t.matmul(bk.t[b][:, :], wg[:, kt, mt * 128:(mt + 1) * 128],
                                  XN[:, kt, n0:n0 + TCH], start=(kt == 0), stop=(kt == NCT - 1))
                bk.wrote(b, ("t", p.inc(ik, "t")))
                bk.rd_wait(s, b)
                ie = s.activation(Gb[:, mt, :], bk.t[b][:, :], AF.Silu)
                bk.read(b, ("s", p.inc(ie, "s")))
                b2 = 2 + mt % 2
                bk.mm_wait(b2)
                for kt in range(NCT):
                    ik = t.matmul(bk.t[b2][:, :], wu[:, kt, mt * 128:(mt + 1) * 128],
                                  XN[:, kt, n0:n0 + TCH], start=(kt == 0), stop=(kt == NCT - 1))
                bk.wrote(b2, ("t", p.inc(ik, "t")))
                bk.rd_wait(v, b2)
                ie = v.tensor_copy(Ub[:, mt, :], bk.t[b2][:, :])
                bk.read(b2, ("v", p.inc(ie, "v")))
            if coef_row is not None:
                bk.mm_wait(4)
                ik = t.matmul(bk.t[4][:, :], onesf[:, :], coef_row[:, n0:n0 + TCH],
                              start=True, stop=True)
                bk.wrote(4, ("t", p.inc(ik, "t")))
                bk.rd_wait(v, 4)
            sv = p.cnt["s"]
            v.wait_ge(p.sem("s"), sv)
            v.wait_ge(p.sem("v"), p.cnt["v"])
            for mt in range(dt):
                iv = v.tensor_mul(Hb[:, mt, :], Gb[:, mt, :], Ub[:, mt, :])
            if coef_row is not None:
                p.esw(iv, "v")
                for mt in range(dt):
                    iv = v.tensor_mul(Hb[:, mt, :], Hb[:, mt, :], bk.t[4][:, :])
            vv = p.esw(iv, "v")
            if coef_row is not None:
                bk.read(4, ("v", vv))
            t.wait_ge(p.sem("v"), vv)
            for mt2 in range(NCT):
                b3 = 5 + mt2 % 2
                bk.mm_wait(b3)
                for kt in range(dt):
                    ik = t.matmul(bk.t[b3][:, :], wd[:, kt, mt2 * 128:(mt2 + 1) * 128],
                                  Hb[:, kt, :], start=(kt == 0), stop=(kt == dt - 1))
                bk.wrote(b3, ("t", p.inc(ik, "t")))
                bk.rd_wait(v, b3)
                if mt2 == 0 and c == NCH - 1 and hasattr(p, "_dps"):
                    v.tensor_copy(p._dps[:, 0, :], bk.t[b3][:, :])
                    v.tensor_copy(p._dps[:, 1, :], Hb[:, 0, :])
                    v.tensor_copy(p._dps[:, 2, :], Hb[:, 7, :])
                iv = v.tensor_add(X[:, mt2, n0:n0 + TCH], X[:, mt2, n0:n0 + TCH], bk.t[b3][:, :])
                bk.read(b3, ("v", p.inc(iv, "v")))

    scS = contextlib.ExitStack()
    wsg, _ = load_w(p, wsg_h, "wsg", D, D, scope=scS)
    wsu, _ = load_w(p, wsu_h, "wsu", D, D, scope=scS)
    wsd, tok = load_w(p, wsd_h, "wsd", D, D, scope=scS)
    t.wait_ge(p.sem("qg"), tok[1])
    path(wsg, wsu, wsd, NCT, None)
    p.barrier("moebar")
    if stop_after == "shared" and hasattr(p, "_moedbg"):
        iv = v.tensor_copy(p._moedbg[:, 0, :], Gb[:, 0, 0:TCH])
        iv = v.tensor_copy(p._moedbg[:, 1, :], Ub[:, 0, 0:TCH])
        iv = v.tensor_copy(p._moedbg[:, 2, :], Hb[:, 0, 0:TCH])
        p.inc(iv, "v")
    if stop_after == "shared":
        scS.close()
        sc.close()
        return ("v", p.cnt["v"])
    scS.close()
    scE = contextlib.ExitStack()
    slots = []
    for i in range(2):
        slots.append((p.sb(f"ewg{i}", [128, 8, 512], BF16, scE),
                      p.sb(f"ewu{i}", [128, 8, 512], BF16, scE),
                      p.sb(f"ewd{i}", [128, 4, D], BF16, scE)))
    slot_free = [None, None]

    def load_expert(e, i):
        wg, wu, wd = slots[i]
        qn = f"qe{i}"
        if slot_free[i] is not None:
            g.wait_ge(p.sem(slot_free[i][0]), slot_free[i][1])
        p.inc(g.dma_start(wg[:, :, :], weg_h[e].rearrange("(kt pp) o -> pp kt o", pp=128)), qn, 16)
        p.inc(g.dma_start(wu[:, :, :], weu_h[e].rearrange("(kt pp) o -> pp kt o", pp=128)), qn, 16)
        i3 = g.dma_start(wd[:, :, :], wed_h[e].rearrange("(kt pp) o -> pp kt o", pp=128))
        return (qn, p.inc(i3, qn, 16))

    ldtok = [load_expert(0, 0), load_expert(1, 1)]
    crtok = [load_crow(0, 0), load_crow(1, 1)]
    for e in range(8):
        i = e % 2
        t.wait_ge(p.sem(crtok[i][0]), crtok[i][1])
        wg, wu, wd = slots[i]
        path(wg, wu, wd, 4, crow[0:1, i, :])
        slot_free[i] = ("t", p.cnt["t"])
        if e + 2 < 8:
            ldtok[i] = load_expert(e + 2, i)
            crtok[i] = load_crow(e + 2, i)
    p.barrier("moebar2")
    scE.close()
    sc.close()
    return ("v", p.cnt["v"])



def emit_l1_attn(p, bk, X, XN, w_h, bc4, sc_cols, xn_tok):
    t, v, s, g, sy = p.engines()
    p.barrier("phb")
    sc = contextlib.ExitStack()
    H8 = p.sb("l1aH", [128, NCT, T], BF16, sc)
    BI = p.sb("l1ai", [128, T], F32, sc)
    BF = p.sb("l1af", [128, T], F32, sc)
    BZ = p.sb("l1az", [128, T], F32, sc)
    B1 = p.sb("l1a1", [128, T], F32, sc)
    BM = p.sb("l1am", [128, T], F32, sc)
    BP = p.sb("l1ap", [128, T], F32, sc)
    rows = p.sb("l1ar", [1, T], F32, sc)
    rows_b = p.sb("l1arb", [1, T], BF16, sc)
    wsl = [p.sb("l1aw0", [128, NCT, 4, 128], BF16, sc)]
    onesb = p.C["ones_col_b"]
    onesr = p.C["ones_row_b"]

    def load_slice(ct):
        ii = g.dma_start(wsl[0][:, :, :, :],
                         w_h[:, ct, :, :].rearrange("(kt pp) gt o -> pp kt gt o", pp=128))
        return ("qa0", p.inc(ii, "qa0", 16))

    ltok = load_slice(0)
    t.wait_ge(p.sem(xn_tok[0]), xn_tok[1])
    for ct in range(NCT):
        t.wait_ge(p.sem(ltok[0]), ltok[1])
        w = wsl[0]
        for c in range(NCH):
            n0 = c * TCH
            for gt in range(4):
                bk.mm_wait(gt)
                for kt in range(NCT):
                    ik = t.matmul(bk.t[gt][:, :], w[:, kt, gt, :], XN[:, kt, n0:n0 + TCH],
                                  start=(kt == 0), stop=(kt == NCT - 1))
                bk.wrote(gt, ("t", p.inc(ik, "t")))
            bk.rd_wait(s, 0)
            ie = s.activation(BI[:, n0:n0 + TCH], bk.t[0][:, :], AF.Tanh,
                              scale=p.cc(0.1), bias=bc4[:, 0, ct:ct + 1])
            bk.read(0, ("s", p.inc(ie, "s")))
            bk.rd_wait(s, 1)
            ie = s.activation(BF[:, n0:n0 + TCH], bk.t[1][:, :], AF.Exp,
                              bias=bc4[:, 1, ct:ct + 1])
            bk.read(1, ("s", p.esw(ie, "s")))
            ie = s.activation(BF[:, n0:n0 + TCH], BF[:, n0:n0 + TCH], AF.Ln,
                              bias=p.cc(1.0))
            p.esw(ie, "s")
            bk.rd_wait(s, 2)
            ie = s.activation(H8[:, ct, n0:n0 + TCH], bk.t[2][:, :], AF.Sigmoid,
                              bias=bc4[:, 2, ct:ct + 1])
            bk.read(2, ("s", p.inc(ie, "s")))
            bk.rd_wait(s, 3)
            ie = s.activation(BZ[:, n0:n0 + TCH], bk.t[3][:, :], AF.Tanh,
                              bias=bc4[:, 3, ct:ct + 1])
            bk.read(3, ("s", p.inc(ie, "s")))
        if ct + 1 < NCT:
            g.wait_ge(p.sem("t"), p.cnt["t"])
            ltok = load_slice(ct + 1)
        sv = p.cnt["s"]
        v.wait_ge(p.sem("s"), sv)
        iv = v.tensor_scalar(BI[:, :], BI[:, :], 10.0, None, ALU.mult)
        iv2 = v.tensor_scalar(BF[:, :], BF[:, :], -1.0, None, ALU.mult)
        p.esw(iv2, "v")
        iv = v.tensor_tensor_scan(BM[:, :], BF[:, :], BI[:, :], 0.0, ALU.add, ALU.max)
        p.esw(iv, "v")
        v.memset(BP[:, 0:1], 0.0)
        iv = v.tensor_copy(BP[:, 1:T], BM[:, 0:T - 1])
        p.esw(iv, "v")
        iv = v.tensor_sub(B1[:, :], BI[:, :], BM[:, :])
        iv2 = v.tensor_sub(BP[:, :], BP[:, :], BM[:, :])
        p.esw(iv2, "v")
        iv = v.tensor_add(BF[:, :], BF[:, :], BP[:, :])
        vv = p.inc(iv, "v")
        s.wait_ge(p.sem("v"), vv)
        ie = s.activation(B1[:, :], B1[:, :], AF.Exp)
        ie2 = s.activation(BF[:, :], BF[:, :], AF.Exp)
        sv = p.inc(ie2, "s")
        v.wait_ge(p.sem("s"), sv)
        iv = v.tensor_mul(BP[:, :], B1[:, :], BZ[:, :])
        p.esw(iv, "v")
        iv = v.tensor_tensor_scan(BI[:, :], BF[:, :], BP[:, :], 0.0, ALU.mult, ALU.add)
        iv = v.tensor_tensor_scan(BZ[:, :], BF[:, :], B1[:, :], 0.0, ALU.mult, ALU.add)
        vv = p.inc(iv, "v")
        s.wait_ge(p.sem("v"), vv)
        ie = s.activation(BF[:, :], BZ[:, :], AF.Ln, bias=p.cc(1e-6))
        p.esw(ie, "s")
        ie = s.activation(BF[:, :], BF[:, :], AF.Exp, scale=p.cc(-1.0))
        sv = p.inc(ie, "s")
        v.wait_ge(p.sem("s"), sv)
        iv = v.tensor_mul(BM[:, :], BI[:, :], BF[:, :])
        p.esw(iv, "v")
        iv = v.tensor_mul(H8[:, ct, :], BM[:, :], H8[:, ct, :])
        vv = p.esw(iv, "v")
        s.wait_ge(p.sem("v"), vv)
        ie = s.activation(B1[:, :], H8[:, ct, :], AF.Square)
        sv = p.inc(ie, "s")
        t.wait_ge(p.sem("s"), sv)
        for c in range(NCH):
            bk.mm_wait(4 + c)
            ik = t.matmul(bk.t[4 + c][0:1, :], p.C["ones_col_f"][:, 0:1],
                          B1[:, c * TCH:(c + 1) * TCH], start=True, stop=True)
            bk.wrote(4 + c, ("t", p.inc(ik, "t")))
        if ct == 0:
            iv = v.memset(rows[:, :], 0.0)
            p.esw(iv, "v")
        for c in range(NCH):
            bk.rd_wait(v, 4 + c)
            iv = v.tensor_add(rows[:, c * TCH:(c + 1) * TCH],
                              rows[:, c * TCH:(c + 1) * TCH], bk.t[4 + c][0:1, :])
            bk.read(4 + c, ("v", p.esw(iv, "v")))
    iv = v.tensor_scalar(rows[:, :], rows[:, :], 1.0 / D, None, ALU.mult)
    p.esw(iv, "v")
    vv = p.cnt["v"]
    s.wait_ge(p.sem("v"), vv)
    ie = s.activation(rows[:, :], rows[:, :], AF.Ln, bias=p.cc(1e-8, 1))
    p.esw(ie, "s")
    ie = s.activation(rows[:, :], rows[:, :], AF.Exp, scale=p.cc(-0.5, 1))
    sv = p.inc(ie, "s")
    v.wait_ge(p.sem("s"), sv)
    iv = v.tensor_copy(rows_b[:, :], rows[:, :])
    vv = p.inc(iv, "v")
    t.wait_ge(p.sem("v"), vv)
    for c in range(NCH):
        n0 = c * TCH
        bk.mm_wait(c)
        ik = t.matmul(bk.t[c][:, :], onesr[:, :], rows_b[:, n0:n0 + TCH],
                      start=True, stop=True)
        bk.wrote(c, ("t", p.inc(ik, "t")))
    for c in range(NCH):
        n0 = c * TCH
        bk.rd_wait(v, c)
        for ct in range(NCT):
            iv = v.tensor_mul(BI[:, 0:TCH], H8[:, ct, n0:n0 + TCH], bk.t[c][:, :])
            p.esw(iv, "v")
            iv = v.scalar_tensor_tensor(X[:, ct, n0:n0 + TCH], BI[:, 0:TCH],
                                        sc_cols[:, ct:ct + 1], X[:, ct, n0:n0 + TCH],
                                        ALU.mult, ALU.add)
            p.esw(iv, "v")
        bk.read(c, ("v", p.cnt["v"]))
    sc.close()
    return ("v", p.cnt["v"])



W1P = 5632
NA = 22
ATTN_LEVEL = 99


def emit_l1_ffn_layer(p, bk, li, XLOC, wqkv_h, wo_h, w1_h, w2_h, tabs, qscols, kscols,
                      qsscols, ksscols, lngb, b1c, b2c, parts=("attn", "ffn")):
    t, v, s, g, sy = p.engines()
    p.barrier("phb")
    if "attn" not in parts:
        return _ffn_only(p, bk, li, XLOC, w1_h, w2_h, lngb, b1c, b2c)
    sc = contextlib.ExitStack()
    XBU = p.sb(f"f{li}xbu", [128, 4, T], BF16, sc)
    XBS = p.sb(f"f{li}xbs", [128, 4, T + 2], BF16, sc)
    KN = p.sb(f"f{li}kn", [128, NCT, 8 * 128], BF16, sc)
    VB = p.sb(f"f{li}vb", [128, 8, D], BF16, sc)
    QN = p.sb(f"f{li}qn", [128, NCT, TCH], BF16, sc)
    QR = p.sb(f"f{li}qr", [128, NCT, TCH], BF16, sc)
    Q2 = p.sb(f"f{li}q2", [128, NCT, TCH], BF16, sc)
    CSQ = p.sb(f"f{li}csq", [128, 4, TCH], BF16, sc)
    wqkv, _ = load_w(p, wqkv_h, f"f{li}wqkv", D, 3072, scope=sc)
    wo, _ = load_w(p, wo_h, f"f{li}wo", D, D, scope=sc)
    rws = p.sb(f"f{li}rw", [16, 3, TCH], F32, sc)
    rwb = p.sb(f"f{li}rwb", [16, TCH], BF16, sc)
    NMC = p.sb(f"f{li}nm", [128, 2, 16], F32, sc)
    LSR = p.sb(f"f{li}lsr", [16, 128], F32, sc)
    PRT = p.sb(f"f{li}prt", [128, 384], BF16, sc)
    AVO = p.sb(f"f{li}avo", [128, NCT, TCH], BF16, sc)
    XLC = p.sb(f"f{li}xlc", [128, 2, TCH], F32, sc)
    hp16 = p.C["hpsel"]; bc16 = p.C["bc16"]; idn = p.C["identity"]
    negi = p.C["negi"]; negsel = p.C["negsel"]
    tq0, tq2 = p.C["tri_q0"], p.C["tri_q2"]; tt0, tt2 = p.C["tri_t0"], p.C["tri_t2"]
    xloc_fm = XLOC.rearrange("(ct pp) t -> pp ct t", pp=128)

    iv = v.memset(XBS[:, :, 0:1], 0.0)
    p.esw(iv, "v")
    g.wait_ge(p.sem("qf"), p.cnt.get("qf", 0))
    ig = g.dma_start(XBU[:, :, :], xloc_fm[:, 0:4, :])
    p.inc(ig, "qf", 16)
    ig = g.dma_start(XBS[:, :, 1:T + 1], xloc_fm[:, 4:8, 0:T])
    xb_tok = ("qf", p.inc(ig, "qf", 16))
    v.wait_ge(p.sem("qs"), p.cnt["qs"])
    v.wait_ge(p.sem("qg"), p.cnt["qg"])
    t.wait_ge(p.sem(xb_tok[0]), xb_tok[1])
    t.wait_ge(p.sem("v"), p.cnt["v"])
    t.wait_ge(p.sem("qg"), p.cnt["qg"])

    def qkv_norm_rot(u, part, dst_fm, n0, depth=3):
        cs = CSQ[:, 0 if part == 0 else 2, :]
        sn = CSQ[:, 1 if part == 0 else 3, :]
        if depth < 1:
            return
        for mt in range(NCT):
            b = mt % 2
            bk.mm_wait(b)
            for kt in range(NCT):
                rhs = (XBU[:, kt, u * TCH:u * TCH + TCH] if kt < 4
                       else XBS[:, kt - 4, u * TCH:u * TCH + TCH])
                ik = t.matmul(bk.t[b][:, :],
                              wqkv[:, kt, part * D + mt * 128:part * D + (mt + 1) * 128],
                              rhs, start=(kt == 0), stop=(kt == NCT - 1))
            bk.wrote(b, ("t", p.inc(ik, "t")))
            bk.rd_wait(v, b)
            iv = v.tensor_copy(QR[:, mt, :], bk.t[b][:, :])
            vv = p.inc(iv, "v")
            bk.read(b, ("v", vv))
            s.wait_ge(p.sem("v"), vv)
            ie = s.activation(Q2[:, mt, :], bk.t[b][:, :], AF.Square)
            bk.read(b, ("s", p.inc(ie, "s")))
        if depth < 2:
            return
        sv = p.cnt["s"]
        t.wait_ge(p.sem("s"), sv)
        bk.mm_wait(4)
        for mt in range(NCT):
            ik = t.matmul(bk.t[4][0:16, :], hp16[:, mt, :], Q2[:, mt, :],
                          start=(mt == 0), stop=(mt == NCT - 1))
        bk.wrote(4, ("t", p.inc(ik, "t")))
        bk.rd_wait(s, 4)
        ie = s.activation(rws[:, 0, :], bk.t[4][0:16, :], AF.Sqrt)
        bk.read(4, ("s", p.esw(ie, "s")))
        sv = p.cnt["s"]
        v.wait_ge(p.sem("s"), sv)
        iv = v.tensor_scalar(rws[:, 1, :], rws[:, 0, :], 1e-12, None, ALU.max)
        vv = p.inc(iv, "v")
        s.wait_ge(p.sem("v"), vv)
        ie = s.activation(rws[:, 2, :], rws[:, 1, :], AF.Ln)
        p.esw(ie, "s")
        ie = s.activation(rws[:, 2, :], rws[:, 2, :], AF.Exp, scale=p.cc(-1.0, 16))
        sv = p.inc(ie, "s")
        v.wait_ge(p.sem("s"), sv)
        iv = v.tensor_copy(rwb[:, :], rws[:, 2, :])
        vv = p.inc(iv, "v")
        if depth < 3:
            return
        t.wait_ge(p.sem("v"), vv)
        for mt in range(NCT):
            bk.mm_wait(2)
            ik = t.matmul(bk.t[2][:, :], p.C["psig"][:, :], QR[:, mt, :], start=True, stop=True)
            bk.wrote(2, ("t", p.inc(ik, "t")))
            bk.mm_wait(3)
            ik = t.matmul(bk.t[3][:, :], bc16[:, mt, :], rwb[:, :], start=True, stop=True)
            bk.wrote(3, ("t", p.inc(ik, "t")))
            bk.rd_wait(v, 2)
            bk.rd_wait(v, 3)
            dst = dst_fm[:, mt, n0:n0 + TCH]
            iv = v.tensor_mul(dst, QR[:, mt, :], cs)
            iv2 = v.tensor_mul(Q2[:, mt, :], bk.t[2][:, :], sn)
            bk.read(2, ("v", p.esw(iv2, "v")))
            iv = v.tensor_add(dst, dst, Q2[:, mt, :])
            p.esw(iv, "v")
            iv = v.tensor_mul(dst, dst, bk.t[3][:, :])
            bk.read(3, ("v", p.inc(iv, "v")))

    for u in range(NCH) if ATTN_LEVEL >= 1 else []:
        n0 = u * TCH
        un = u * TCH
        v.wait_ge(p.sem("t"), p.cnt["t"])
        iv = v.tensor_scalar(CSQ[:, 0, :], tabs["cs"][:, un:un + TCH], qscols[:, li:li + 1], 8.0, ALU.mult, ALU.mult)
        iv = v.tensor_scalar(CSQ[:, 1, :], tabs["snp"][:, un:un + TCH], qsscols[:, li:li + 1], 8.0, ALU.mult, ALU.mult)
        iv = v.tensor_scalar(CSQ[:, 2, :], tabs["ci"][:, un:un + TCH], kscols[:, li:li + 1], None, ALU.mult)
        iv = v.tensor_scalar(CSQ[:, 3, :], tabs["sip"][:, un:un + TCH], ksscols[:, li:li + 1], None, ALU.mult)
        vv = p.esw(iv, "v")
        qkv_norm_rot(u, 1, KN, (u % 2) * TCH, depth=min(3, ATTN_LEVEL - 1))
        if ATTN_LEVEL >= 4:
            qkv_norm_rot(u, 0, QN, 0)
        for tt4 in range(4) if ATTN_LEVEL >= 3 else []:
            tt = u * 4 + tt4
            for half in range(2):
                b = 5 + half
                bk.mm_wait(b)
                for kt in range(NCT):
                    lhs = (XBU[:, kt, tt * 128:tt * 128 + 128] if kt < 4
                           else XBS[:, kt - 4, tt * 128:tt * 128 + 128])
                    ik = t.matmul(bk.t[b][:, :], lhs,
                                  wqkv[:, kt, 2 * D + half * 512:2 * D + (half + 1) * 512],
                                  start=(kt == 0), stop=(kt == NCT - 1))
                bk.wrote(b, ("t", p.inc(ik, "t")))
                bk.rd_wait(v, b)
                iv = v.tensor_copy(VB[:, tt % 8, half * 512:(half + 1) * 512], bk.t[b][:, :])
                bk.read(b, ("v", p.inc(iv, "v")))
        for qt4 in range(4) if ATTN_LEVEL >= 4 else []:
            qt = u * 4 + qt4
            r0 = max(0, qt - 2)
            nk = qt - r0 + 1
            q0 = qt4 * 128
            for h in range(H):
                pb = 64 * (h % 2)
                bk.mm_wait(0)
                first = True
                for r in range(r0, qt + 1):
                    cc0 = (r - r0) * 128
                    if r == qt - 2:
                        t.matmul(bk.t[0][:, cc0:cc0 + 128], negi[:, :], tq0[:, :],
                                 start=first, stop=False)
                        first = False
                    if r == qt:
                        t.matmul(bk.t[0][:, cc0:cc0 + 128], negi[:, :], tq2[:, :],
                                 start=first, stop=False)
                        first = False
                for r in range(r0, qt + 1):
                    cc0 = (r - r0) * 128
                    ik = t.matmul(bk.t[0][:, cc0:cc0 + 128],
                                  QN[pb:pb + 64, h // 2, q0:q0 + 128],
                                  KN[pb:pb + 64, h // 2, (r % 8) * 128:(r % 8) * 128 + 128],
                                  start=first, stop=(r == qt), tile_position=(pb, 0))
                    first = False
                bk.wrote(0, ("t", p.inc(ik, "t")))
                bk.rd_wait(v, 0)
                iv = v.reduce_max(NMC[:, 0, h:h + 1], bk.t[0][:, 0:nk * 128],
                                  axis=AX.X, negate=True)
                vv = p.esw(iv, "v")
                s.wait_ge(p.sem("v"), vv)
                ie = s.activation(bk.t[2][:, 0:nk * 128], bk.t[0][:, 0:nk * 128], AF.Exp,
                                  bias=NMC[:, 0, h:h + 1], accum_out=NMC[:, 1, h:h + 1])
                bk.read(0, ("s", p.esw(ie, "s")))
                bk.read(2, ("s", p.cnt["s"]))
                ie = s.activation(NMC[:, 1, h:h + 1], NMC[:, 1, h:h + 1], AF.Ln)
                sv = p.esw(ie, "s")
                v.wait_ge(p.sem("s"), sv)
                iv = v.tensor_sub(NMC[:, 1, h:h + 1], NMC[:, 1, h:h + 1], NMC[:, 0, h:h + 1])
                p.esw(iv, "v")
            if ATTN_LEVEL < 5:
                continue
            vv = p.cnt["v"]
            t.wait_ge(p.sem("v"), vv)
            bk.mm_wait(1)
            ik = t.transpose(bk.t[1][0:16, 0:128], NMC[:, 1, :], idn[:, :])
            bk.wrote(1, ("t", p.inc(ik, "t")))
            bk.rd_wait(v, 1)
            iv = v.tensor_copy(LSR[:, :], bk.t[1][0:16, 0:128])
            bk.read(1, ("v", p.esw(iv, "v")))
            vv = p.cnt["v"]
            t.wait_ge(p.sem("v"), vv)
            for h in range(H):
                pb = 64 * (h % 2)
                bk.mm_wait(1)
                first = True
                for r in range(r0, qt + 1):
                    cc0 = (r - r0) * 128
                    if r == qt - 2:
                        t.matmul(bk.t[1][:, cc0:cc0 + 128], negi[:, :], tt0[:, :],
                                 start=first, stop=False)
                        first = False
                    if r == qt:
                        t.matmul(bk.t[1][:, cc0:cc0 + 128], negi[:, :], tt2[:, :],
                                 start=first, stop=False)
                        first = False
                for r in range(r0, qt + 1):
                    cc0 = (r - r0) * 128
                    t.matmul(bk.t[1][:, cc0:cc0 + 128],
                             KN[pb:pb + 64, h // 2, (r % 8) * 128:(r % 8) * 128 + 128],
                             QN[pb:pb + 64, h // 2, q0:q0 + 128],
                             start=first, stop=False, tile_position=(pb, 0))
                    first = False
                    ik = t.matmul(bk.t[1][:, cc0:cc0 + 128], negsel[:, h, :],
                                  LSR[:, :], start=False, stop=(r == qt))
                bk.wrote(1, ("t", p.inc(ik, "t")))
                bk.rd_wait(s, 1)
                ie = s.activation(PRT[:, 0:nk * 128], bk.t[1][:, 0:nk * 128], AF.Exp)
                bk.read(1, ("s", p.inc(ie, "s")))
                sv = p.cnt["s"]
                t.wait_ge(p.sem("s"), sv)
                if h % 2 == 0:
                    bk.mm_wait(3)
                for r in range(r0, qt + 1):
                    ik = t.matmul(bk.t[3][pb:pb + 64, q0:q0 + 128],
                                  VB[:, r % 8, h * 64:h * 64 + 64],
                                  PRT[:, (r - r0) * 128:(r - r0 + 1) * 128],
                                  start=(r == r0), stop=(r == qt),
                                  tile_position=(0, pb))
                bk.wrote(3, ("t", p.inc(ik, "t")))
                if h % 2 == 1:
                    bk.rd_wait(v, 3)
                    iv = v.tensor_copy(AVO[:, h // 2, q0:q0 + 128], bk.t[3][:, q0:q0 + 128])
                    bk.read(3, ("v", p.inc(iv, "v")))
        vv = p.cnt["v"]
        t.wait_ge(p.sem("v"), vv)
        for mt in range(NCT) if ATTN_LEVEL >= 6 else []:
            sl = mt % 2
            if p.cnt.get("qf", 0) > 0:
                sy.wait_ge(p.sem("qf"), p.cnt["qf"])
            il = sy.dma_start(XLC[:, sl, :], xloc_fm[:, mt, n0:n0 + TCH])
            xl_tok = (f"qx{sl}", p.inc(il, f"qx{sl}", 16))
            b = 5 + mt % 2
            bk.mm_wait(b)
            for kt in range(NCT):
                ik = t.matmul(bk.t[b][:, :], wo[:, kt, mt * 128:(mt + 1) * 128],
                              AVO[:, kt, :], start=(kt == 0), stop=(kt == NCT - 1))
            bk.wrote(b, ("t", p.inc(ik, "t")))
            bk.rd_wait(v, b)
            v.wait_ge(p.sem(xl_tok[0]), xl_tok[1])
            iv = v.tensor_add(XLC[:, sl, :], XLC[:, sl, :], bk.t[b][:, :])
            bk.read(b, ("v", p.inc(iv, "v")))
            vv = p.cnt["v"]
            g.wait_ge(p.sem("v"), vv)
            ig = g.dma_start(xloc_fm[:, mt, n0:n0 + TCH], XLC[:, sl, :])
            p.inc(ig, "qf", 16)
        g.wait_ge(p.sem("qf"), p.cnt["qf"])
    p.barrier("phb2")
    sc.close()
    if "ffn" not in parts:
        return ("v", p.cnt["v"])
    return _ffn_only(p, bk, li, XLOC, w1_h, w2_h, lngb, b1c, b2c)


def _ffn_only(p, bk, li, XLOC, w1_h, w2_h, lngb, b1c, b2c):
    t, v, s, g, sy = p.engines()
    xloc_fm = XLOC.rearrange("(ct pp) t -> pp ct t", pp=128)
    sc2 = contextlib.ExitStack()
    XC = p.sb(f"g{li}xc", [128, NCT, TCH + 1], F32, sc2)
    XN2 = p.sb(f"g{li}xn2", [128, NCT, TCH], BF16, sc2)
    TB1 = p.sb(f"g{li}tb1", [128, NCT, TCH], BF16, sc2)
    TB2 = p.sb(f"g{li}tb2", [128, NCT, TCH], BF16, sc2)
    AB = p.sb(f"g{li}ab", [128, NA, TCH], BF16, sc2)
    GB = p.sb(f"g{li}gb", [128, NA, TCH], BF16, sc2)
    OUTB = p.sb(f"g{li}out", [128, NCT, TCH], F32, sc2)
    lrow = p.sb(f"g{li}lr", [1, 4, TCH], F32, sc2)
    lrow_b = p.sb(f"g{li}lrb", [1, 2, TCH], BF16, sc2)
    w1sl = [p.sb(f"g{li}w1{i}", [128, NCT, 512], BF16, sc2) for i in range(2)]
    w2f, w2_tok = load_w(p, w2_h, f"g{li}w2", 2816, D, scope=sc2)
    onesb = p.C["ones_col_b"]
    w1rd = [0, 0]

    def load_w1(mg, i):
        qn = f"qw{i}"
        g.wait_ge(p.sem("t"), w1rd[i])
        ii = g.dma_start(w1sl[i][:, :, :],
                         w1_h[:, mg * 512:(mg + 1) * 512].rearrange("(kt pp) o -> pp kt o", pp=128))
        return (qn, p.inc(ii, qn, 16))

    for c in range(NCH):
        n0 = c * TCH
        w1tok = [load_w1(0, 0), load_w1(1, 1)]
        if c > 0:
            sy.wait_ge(p.sem("v"), p.cnt["v"])
            sy.wait_ge(p.sem("s"), p.cnt["s"])
        sy.wait_ge(p.sem("qf"), p.cnt.get("qf", 0))
        if c == 0:
            iv = v.memset(XC[:, :, 0:1], 0.0)
            p.esw(iv, "v")
            il = sy.dma_start(XC[:, :, 1:TCH + 1], xloc_fm[:, :, 0:TCH])
        else:
            il = sy.dma_start(XC[:, :, :], xloc_fm[:, :, n0 - 1:n0 + TCH])
        xc_tok = ("qx", p.inc(il, "qx", 16))
        for e in (t, v, s):
            e.wait_ge(p.sem(xc_tok[0]), xc_tok[1])
        if c == 0:
            t.wait_ge(p.sem("v"), p.cnt["v"])
            s.wait_ge(p.sem("v"), p.cnt["v"])

        def xs2(ct):
            off = 1 if ct < 4 else 0
            return XC[:, ct, off:off + TCH]

        for ct in range(NCT):
            iv = v.tensor_copy(TB1[:, ct, :], xs2(ct))
            ie = s.activation(TB2[:, ct, :], xs2(ct), AF.Square)
        vv = p.inc(iv, "v"); sv = p.inc(ie, "s")
        t.wait_ge(p.sem("v"), vv); t.wait_ge(p.sem("s"), sv)
        bk.mm_wait(0); bk.mm_wait(1)
        for ct in range(NCT):
            t.matmul(bk.t[0][0:1, :], onesb[:, 0:1], TB1[:, ct, :], start=(ct == 0), stop=(ct == NCT - 1))
        for ct in range(NCT):
            ik = t.matmul(bk.t[1][0:1, :], onesb[:, 0:1], TB2[:, ct, :], start=(ct == 0), stop=(ct == NCT - 1))
        tv = p.inc(ik, "t")
        bk.wrote(0, ("t", tv)); bk.wrote(1, ("t", tv))
        bk.rd_wait(v, 0); bk.rd_wait(v, 1)
        v.tensor_scalar(lrow[:, 0, :], bk.t[0][0:1, :], 1.0 / D, None, ALU.mult)
        iv = v.tensor_scalar(lrow[:, 1, :], bk.t[1][0:1, :], 1.0 / D, None, ALU.mult)
        bk.read(0, ("v", p.esw(iv, "v"))); bk.read(1, ("v", p.cnt["v"]))
        iv = v.tensor_mul(lrow[:, 2, :], lrow[:, 0, :], lrow[:, 0, :])
        p.esw(iv, "v")
        iv = v.tensor_sub(lrow[:, 3, :], lrow[:, 1, :], lrow[:, 2, :])
        vv = p.inc(iv, "v")
        s.wait_ge(p.sem("v"), vv)
        ie = s.activation(lrow[:, 3, :], lrow[:, 3, :], AF.Ln, bias=p.cc(1e-5, 1))
        p.esw(ie, "s")
        ie = s.activation(lrow[:, 3, :], lrow[:, 3, :], AF.Exp, scale=p.cc(-0.5, 1))
        sv = p.inc(ie, "s")
        v.wait_ge(p.sem("s"), sv)
        v.tensor_copy(lrow_b[:, 0, :], lrow[:, 0, :])
        iv = v.tensor_copy(lrow_b[:, 1, :], lrow[:, 3, :])
        vv = p.inc(iv, "v")
        t.wait_ge(p.sem("v"), vv)
        bk.mm_wait(0); bk.mm_wait(1)
        t.matmul(bk.t[0][:, :], p.C["ones_row_b"][:, :], lrow_b[:, 0, :], start=True, stop=True)
        ik = t.matmul(bk.t[1][:, :], p.C["ones_row_b"][:, :], lrow_b[:, 1, :], start=True, stop=True)
        tv = p.inc(ik, "t")
        bk.wrote(0, ("t", tv)); bk.wrote(1, ("t", tv))
        bk.rd_wait(v, 0); bk.rd_wait(v, 1)
        for ct in range(NCT):
            iv = v.tensor_sub(TB1[:, ct, :], xs2(ct), bk.t[0][:, :])
        p.esw(iv, "v")
        bk.read(0, ("v", p.cnt["v"]))
        for ct in range(NCT):
            iv = v.tensor_mul(TB2[:, ct, :], TB1[:, ct, :], bk.t[1][:, :])
        p.esw(iv, "v")
        bk.read(1, ("v", p.cnt["v"]))
        for ct in range(NCT):
            iv = v.tensor_scalar(XN2[:, ct, :], TB2[:, ct, :], lngb[:, 0, ct, li:li + 1],
                                 lngb[:, 1, ct, li:li + 1], ALU.mult, ALU.add)
        vv = p.inc(iv, "v")
        t.wait_ge(p.sem("v"), vv)
        for mg in range(11):
            i_ = mg % 2
            t.wait_ge(p.sem(w1tok[i_][0]), w1tok[i_][1])
            for m4 in range(4):
                m = mg * 4 + m4
                b = 2 + m4 % 2
                bk.mm_wait(b)
                for kt in range(NCT):
                    ik = t.matmul(bk.t[b][:, :], w1sl[i_][:, kt, m4 * 128:(m4 + 1) * 128],
                                  XN2[:, kt, :], start=(kt == 0), stop=(kt == NCT - 1))
                bk.wrote(b, ("t", p.inc(ik, "t")))
                if m < NA:
                    bk.rd_wait(v, b)
                    iv = v.tensor_scalar(AB[:, m, :], bk.t[b][:, :], b1c[:, m, li:li + 1],
                                         None, ALU.add)
                    bk.read(b, ("v", p.inc(iv, "v")))
                else:
                    bk.rd_wait(s, b)
                    ie = s.activation(GB[:, m - NA, :], bk.t[b][:, :], AF.Gelu,
                                      bias=b1c[:, m, li:li + 1])
                    bk.read(b, ("s", p.inc(ie, "s")))
            w1rd[i_] = p.cnt["t"]
            if mg + 2 < 11:
                w1tok[i_] = load_w1(mg + 2, i_)
        vv = p.cnt["v"]; sv = p.cnt["s"]
        v.wait_ge(p.sem("s"), sv)
        v.wait_ge(p.sem("v"), vv)
        for m in range(NA):
            iv = v.tensor_mul(AB[:, m, :], AB[:, m, :], GB[:, m, :])
        vv = p.esw(iv, "v")
        t.wait_ge(p.sem("v"), vv)
        t.wait_ge(p.sem("qg"), w2_tok[1])
        for mt in range(NCT):
            b = 4 + mt % 2
            bk.mm_wait(b)
            for kt in range(NA):
                ik = t.matmul(bk.t[b][:, :], w2f[:, kt, mt * 128:(mt + 1) * 128],
                              AB[:, kt, :], start=(kt == 0), stop=(kt == NA - 1))
            bk.wrote(b, ("t", p.inc(ik, "t")))
            bk.rd_wait(v, b)
            iv = v.scalar_tensor_tensor(OUTB[:, mt, :], bk.t[b][:, :], b2c[:, mt, li:li + 1],
                                        XC[:, mt, 1:TCH + 1], ALU.add, ALU.add)
            bk.read(b, ("v", p.inc(iv, "v")))
        vv = p.cnt["v"]
        g.wait_ge(p.sem("v"), vv)
        ig = g.dma_start(xloc_fm[:, :, n0:n0 + TCH], OUTB[:, :, :])
        p.inc(ig, "qf", 16)
        g.wait_ge(p.sem("qf"), p.cnt["qf"])
    p.barrier("phb3")
    sc2.close()
    return ("v", p.cnt["v"])



def _host_prep(inp):
    H_ = {}
    half = np.arange(0, 64, 2, dtype=np.float64)
    inv_freq = 1.0 / 10000.0 ** (half / 64)
    tt_ = np.arange(T, dtype=np.float64)
    freqs = np.concatenate([tt_[:, None] * inv_freq[None, :]] * 2, -1)
    base_scale = (half + 0.4 * 64) / (1.4 * 64)
    power = (tt_ - T // 2) / 128
    scl = np.concatenate([base_scale[None, :] ** power[:, None]] * 2, -1)
    cosf, sinf = np.cos(freqs), np.sin(freqs)
    sgn = np.where(np.arange(64) < 32, -1.0, 1.0)
    H_["cs"] = np.tile((cosf * scl).T, (2, 1)).astype(np.float32)
    H_["snp"] = np.tile((sinf * scl).T * sgn[:, None], (2, 1)).astype(np.float32)
    H_["ci"] = np.tile((cosf / scl).T, (2, 1)).astype(np.float32)
    H_["sip"] = np.tile((sinf / scl).T * sgn[:, None], (2, 1)).astype(np.float32)
    QS, KS = inp["L1_ffn_q_scale"], inp["L1_ffn_k_scale"]
    si = (np.arange(64) + 32) % 64
    H_["qsc"] = np.tile(QS.T, (2, 1)).astype(np.float32)
    H_["ksc"] = np.tile(KS.T, (2, 1)).astype(np.float32)
    H_["qssc"] = np.tile(QS[:, si].T, (2, 1)).astype(np.float32)
    H_["kssc"] = np.tile(KS[:, si].T, (2, 1)).astype(np.float32)
    W1, W2, B1, B2 = (inp["L1_ffn_W1"], inp["L1_ffn_W2"], inp["L1_ffn_b1"], inp["L1_ffn_b2"])
    W1p = np.zeros((6, 1024, 5632), np.float32)
    W1p[:, :, 0:2730] = W1[:, :, 0:2730]; W1p[:, :, 2816:2816 + 2730] = W1[:, :, 2730:]
    W2p = np.zeros((6, 2816, 1024), np.float32)
    W2p[:, 0:2730, :] = W2
    b1p = np.zeros((6, 5632), np.float32)
    b1p[:, 0:2730] = B1[:, 0:2730]; b1p[:, 2816:2816 + 2730] = B1[:, 2730:]
    H_["w1p"] = W1p; H_["w2p"] = W2p
    H_["b1c"] = np.ascontiguousarray(b1p.reshape(6, 44, 128).transpose(2, 1, 0))
    H_["b2c"] = np.ascontiguousarray(B2.reshape(6, 8, 128).transpose(2, 1, 0))
    H_["lngb"] = np.ascontiguousarray(np.stack([inp["L1_ffn_ln_g"], inp["L1_ffn_ln_b"]], 0)
                                      .reshape(2, 6, 8, 128).transpose(3, 0, 2, 1))
    H_["wl1"] = np.ascontiguousarray(inp["L1_attn_W"].reshape(1024, 4, 8, 128).transpose(0, 2, 1, 3))
    H_["bc4"] = np.ascontiguousarray(inp["L1_attn_b"].reshape(4, 8, 128).transpose(2, 0, 1))
    H_["inb"] = np.ascontiguousarray(inp["L1_attn_input_bias"].reshape(8, 128).T)
    H_["sclc"] = np.ascontiguousarray(inp["L1_attn_scale"].reshape(8, 128).T)
    H_["gbc"] = np.stack([inp["L0_attn_gamma"].reshape(8, 128).T,
                          inp["L0_attn_beta"].reshape(8, 128).T], 1)
    return H_


def _launch1(inp, xs):
    p = Prog()
    x_h = p.din("x", xs, percore=True)
    wq_h = p.din("wq", inp["L0_attn_W_Q"]); wk_h = p.din("wk", inp["L0_attn_W_K"])
    wv_h = p.din("wv", inp["L0_attn_W_V"]); wfc_h = p.din("wfc", inp["L0_attn_W_fc"])
    gb_h = p.din("gbcols", _host_prep.H["gbc"])
    rt_h = p.din("rt", inp["L0_ffn_router"])
    weg_h = p.din("weg", inp["L0_ffn_experts_gate"]); weu_h = p.din("weu", inp["L0_ffn_experts_up"])
    wed_h = p.din("wed", inp["L0_ffn_experts_down"])
    wsg_h = p.din("wsg", inp["L0_ffn_shared_gate"]); wsu_h = p.din("wsu", inp["L0_ffn_shared_up"])
    wsd_h = p.din("wsd", inp["L0_ffn_shared_down"])
    y_h = p.dout("xout", [128, NCT, T], F32)
    load_consts(p, ["cc", "ones_col_b", "ones_row_b", "ones_row_f", "ones_col_f",
                    "identity", "hpsel", "bc16"])
    bkp = Banks(p)
    X = p.sb("X", [128, NCT, T], F32)
    XN = p.sb("XN", [128, NCT, T], BF16)
    gb = p.sb("gb", [128, 2, NCT], F32)
    t, v, s, g, sy = p.engines()
    emit_load_x(p, x_h, X)
    p.inc(sy.dma_start(gb[...], gb_h), "qs", 16)
    for e in (t, v, s):
        e.wait_ge(p.sem("qs"), p.cnt["qs"])
    vtok = emit_ln(p, bkp, X, XN)
    vtok = emit_l0_attn(p, bkp, X, XN, wq_h, wk_h, wv_h, wfc_h, gb[:, 0, :], gb[:, 1, :], vtok)
    vtok = emit_l0_ffn(p, bkp, X, XN, rt_h, weg_h, weu_h, wed_h, wsg_h, wsu_h, wsd_h, vtok)
    sy.wait_ge(p.sem(vtok[0]), vtok[1])
    sy.wait_ge(p.sem("t"), p.cnt["t"])
    st = p.inc(sy.dma_start(y_h, X[:, :, :]), "qs", 16)
    sy.wait_ge(p.sem("qs"), st)
    finish(p)
    return [r["xout"] for r in run_program(p)]


def _launch2(inp, x1s):
    H_ = _host_prep.H
    p = Prog()
    x_h = p.din("x", x1s, percore=True)
    w_h = p.din("l1w", H_["wl1"])
    bc_h = p.din("l1bc", H_["bc4"]); inb_h = p.din("l1inb", H_["inb"])
    scl_h = p.din("l1scl", H_["sclc"])
    y_h = p.dout("xout", [128, NCT, T], F32)
    load_consts(p, ["cc", "ones_col_b", "ones_row_b", "ones_col_f", "ones_row_f"])
    bkp = Banks(p)
    X = p.sb("X", [128, NCT, T], F32)
    XN = p.sb("XN", [128, NCT, T], BF16)
    bc4 = p.sb("bc4", [128, 4, NCT], F32)
    inb = p.sb("inb", [128, NCT], F32)
    scl = p.sb("scl", [128, NCT], F32)
    t, v, s, g, sy = p.engines()
    p.inc(sy.dma_start(X[...], x_h), "qs", 16)
    p.inc(sy.dma_start(bc4[...], bc_h), "qs", 16)
    p.inc(sy.dma_start(inb[...], inb_h), "qs", 16)
    p.inc(sy.dma_start(scl[...], scl_h), "qs", 16)
    for e in (t, v, s):
        e.wait_ge(p.sem("qs"), p.cnt["qs"])
    iv = v.tensor_add(bc4[:, 0, :], bc4[:, 0, :], inb[:, :])
    p.esw(iv, "v")
    iv = v.tensor_scalar(bc4[:, 0, :], bc4[:, 0, :], 0.1, None, ALU.mult)
    p.esw(iv, "v")
    vtok = emit_ln(p, bkp, X, XN)
    vtok = emit_l1_attn(p, bkp, X, XN, w_h, bc4, scl, vtok)
    sy.wait_ge(p.sem(vtok[0]), vtok[1])
    sy.wait_ge(p.sem("t"), p.cnt["t"])
    st = p.inc(sy.dma_start(y_h, X[:, :, :]), "qs", 16)
    sy.wait_ge(p.sem("qs"), st)
    finish(p)
    res = run_program(p)
    return [r["xout"] for r in res]


def _launch3(inp, x2s, x1s):
    H_ = _host_prep.H
    p = Prog()
    x2_h = p.din("x2", x2s, percore=True)
    x1_h = p.din("x1", x1s, percore=True)
    wqkv_h = p.din("wqkv", inp["L1_ffn_Wqkv"]); wo_h = p.din("wo6", inp["L1_ffn_Wo"])
    w1_h = p.din("w1p", H_["w1p"]); w2_h = p.din("w2p", H_["w2p"])
    tb_h = {k: p.din("tab_" + k, H_[k]) for k in ["cs", "snp", "ci", "sip"]}
    qs_h = p.din("qsc", H_["qsc"]); ks_h = p.din("ksc", H_["ksc"])
    qss_h = p.din("qssc", H_["qssc"]); kss_h = p.din("kssc", H_["kssc"])
    lngb_h = p.din("lngbc", H_["lngb"]); b1c_h = p.din("b1cc", H_["b1c"])
    b2c_h = p.din("b2cc", H_["b2c"])
    y_h = p.dout("y", [128, NCT, T], F32)
    load_consts(p, ["cc", "ones_col_b", "ones_row_b", "ones_col_f", "ones_row_f",
                    "identity", "hpsel", "bc16", "psig", "negsel", "negi",
                    "tri_q0", "tri_q2", "tri_t0", "tri_t2"])
    bkp = Banks(p)
    t, v, s, g, sy = p.engines()
    tabs = {}
    for k in ["cs", "snp", "ci", "sip"]:
        tabs[k] = p.sb("tb_" + k, [128, T], BF16)
        p.inc(g.dma_start(tabs[k][...], tb_h[k]), "qg", 16)
    cols = {}
    for k, h in [("qs", qs_h), ("ks", ks_h), ("qss", qss_h), ("kss", kss_h)]:
        cols[k] = p.sb("cl_" + k, [128, 6], F32)
        p.inc(sy.dma_start(cols[k][...], h), "qs", 16)
    lngb_s = p.sb("lngbs", [128, 2, 8, 6], F32)
    p.inc(sy.dma_start(lngb_s[...], lngb_h), "qs", 16)
    b1c_s = p.sb("b1cs", [128, 44, 6], F32)
    p.inc(sy.dma_start(b1c_s[...], b1c_h), "qs", 16)
    b2c_s = p.sb("b2cs", [128, 8, 6], F32)
    p.inc(sy.dma_start(b2c_s[...], b2c_h), "qs", 16)
    XLOC = p.dram("xloc", [D, T], F32)
    scL = contextlib.ExitStack()
    X1 = p.sb("X1", [128, NCT, T], F32, scL)
    XNB = p.sb("XNB", [128, NCT, T], BF16, scL)
    p.inc(sy.dma_start(X1[...], x1_h), "qs", 16)
    for e in (t, v, s):
        e.wait_ge(p.sem("qs"), p.cnt["qs"])
    vtok = emit_ln(p, bkp, X1, XNB, xn_f32=X1)
    g.wait_ge(p.sem(vtok[0]), vtok[1])
    ii = g.dma_start(XLOC.rearrange("(ct pp) t -> pp ct t", pp=128), X1[:, :, :])
    p.inc(ii, "qf", 16)
    g.wait_ge(p.sem("qf"), p.cnt["qf"])
    g.wait_ge(p.sem("qg"), p.cnt["qg"])
    p.barrier("lnfree")
    scL.close()
    for li in range(6):
        emit_l1_ffn_layer(p, bkp, li, XLOC, wqkv_h[li], wo_h[li], w1_h[li], w2_h[li], tabs,
                          cols["qs"], cols["ks"], cols["qss"], cols["kss"],
                          lngb_s, b1c_s, b2c_s)
    p.barrier("fin")
    FA = p.sb("finA", [128, NCT, TCH], F32)
    FB = p.sb("finB", [128, NCT, TCH], F32)
    xloc_fm = XLOC.rearrange("(ct pp) t -> pp ct t", pp=128)
    for c in range(NCH):
        n0 = c * TCH
        if c > 0:
            sy.wait_ge(p.sem("v"), p.cnt["v"])
            sy.wait_ge(p.sem("qy"), p.cnt["qy"])
        il = sy.dma_start(FA[:, :, :], x2_h[:, :, n0:n0 + TCH])
        p.inc(il, "qs", 16)
        il = sy.dma_start(FB[:, :, :], xloc_fm[:, :, n0:n0 + TCH])
        tokf = p.inc(il, "qs", 16)
        v.wait_ge(p.sem("qs"), tokf)
        for ct in range(NCT):
            iv = v.tensor_add(FA[:, ct, :], FA[:, ct, :], FB[:, ct, :])
        vv = p.esw(iv, "v")
        g.wait_ge(p.sem("v"), vv)
        ig = g.dma_start(y_h[:, :, n0:n0 + TCH], FA[:, :, :])
        p.inc(ig, "qy", 16)
        g.wait_ge(p.sem("qy"), p.cnt["qy"])
    finish(p)
    return [r["y"] for r in run_program(p)]


def _launch_fused(inp, xs):
    H_ = _host_prep.H
    p = Prog()
    x_h = p.din("x", xs, percore=True)
    wq_h = p.din("wq", inp["L0_attn_W_Q"]); wk_h = p.din("wk", inp["L0_attn_W_K"])
    wv_h = p.din("wv", inp["L0_attn_W_V"]); wfc_h = p.din("wfc", inp["L0_attn_W_fc"])
    gb_h = p.din("gbcols", H_["gbc"])
    rt_h = p.din("rt", inp["L0_ffn_router"])
    weg_h = p.din("weg", inp["L0_ffn_experts_gate"]); weu_h = p.din("weu", inp["L0_ffn_experts_up"])
    wed_h = p.din("wed", inp["L0_ffn_experts_down"])
    wsg_h = p.din("wsg", inp["L0_ffn_shared_gate"]); wsu_h = p.din("wsu", inp["L0_ffn_shared_up"])
    wsd_h = p.din("wsd", inp["L0_ffn_shared_down"])
    wl1_h = p.din("l1w", H_["wl1"])
    bc_h = p.din("l1bc", H_["bc4"]); inb_h = p.din("l1inb", H_["inb"])
    scl_h = p.din("l1scl", H_["sclc"])
    wqkv_h = p.din("wqkv", inp["L1_ffn_Wqkv"]); wo_h = p.din("wo6", inp["L1_ffn_Wo"])
    w1_h = p.din("w1p", H_["w1p"]); w2_h = p.din("w2p", H_["w2p"])
    tb_h = {k: p.din("tab_" + k, H_[k]) for k in ["cs", "snp", "ci", "sip"]}
    qs_h = p.din("qsc", H_["qsc"]); ks_h = p.din("ksc", H_["ksc"])
    qss_h = p.din("qssc", H_["qssc"]); kss_h = p.din("kssc", H_["kssc"])
    lngb_h = p.din("lngbc", H_["lngb"]); b1c_h = p.din("b1cc", H_["b1c"])
    b2c_h = p.din("b2cc", H_["b2c"])
    y_h = p.dout("y", [128, NCT, T], F32)
    load_consts(p, ["cc", "ones_col_b", "ones_row_b", "ones_row_f", "ones_col_f",
                    "identity", "hpsel", "bc16"])
    bkp = Banks(p)
    t, v, s, g, sy = p.engines()
    XLOC = p.dram("xloc", [D, T], F32)
    X2S = p.dram("x2scr", [D, T], F32)
    gb = p.sb("gb", [128, 2, NCT], F32)
    bc4 = p.sb("bc4", [128, 4, NCT], F32)
    inb = p.sb("inb", [128, NCT], F32)
    scl = p.sb("scl", [128, NCT], F32)
    scX = contextlib.ExitStack()
    X = p.sb("X", [128, NCT, T], F32, scX)
    XN = p.sb("XN", [128, NCT, T], BF16, scX)
    emit_load_x(p, x_h, X)
    p.inc(sy.dma_start(gb[...], gb_h), "qs", 16)
    p.inc(sy.dma_start(bc4[...], bc_h), "qs", 16)
    p.inc(sy.dma_start(inb[...], inb_h), "qs", 16)
    p.inc(sy.dma_start(scl[...], scl_h), "qs", 16)
    for e in (t, v, s):
        e.wait_ge(p.sem("qs"), p.cnt["qs"])
    iv = v.tensor_add(bc4[:, 0, :], bc4[:, 0, :], inb[:, :])
    p.esw(iv, "v")
    iv = v.tensor_scalar(bc4[:, 0, :], bc4[:, 0, :], 0.1, None, ALU.mult)
    p.esw(iv, "v")
    vtok = emit_ln(p, bkp, X, XN)
    vtok = emit_l0_attn(p, bkp, X, XN, wq_h, wk_h, wv_h, wfc_h, gb[:, 0, :], gb[:, 1, :], vtok)
    vtok = emit_l0_ffn(p, bkp, X, XN, rt_h, weg_h, weu_h, wed_h, wsg_h, wsu_h, wsd_h, vtok)
    p.barrier("phF1")
    scN = contextlib.ExitStack()
    XNF = p.sb("XNF", [128, NCT, T], F32, scN)
    vtok = emit_ln(p, bkp, X, XN, xn_f32=XNF, src_tok=vtok)
    g.wait_ge(p.sem(vtok[0]), vtok[1])
    ii = g.dma_start(XLOC.rearrange("(ct pp) t -> pp ct t", pp=128), XNF[:, :, :])
    p.inc(ii, "qf", 16)
    g.wait_ge(p.sem("qf"), p.cnt["qf"])
    p.barrier("phF2")
    scN.close()
    vtok = emit_l1_attn(p, bkp, X, XN, wl1_h, bc4, scl, vtok)
    sy.wait_ge(p.sem(vtok[0]), vtok[1])
    sy.wait_ge(p.sem("t"), p.cnt["t"])
    ii = sy.dma_start(X2S.rearrange("(ct pp) t -> pp ct t", pp=128), X[:, :, :])
    p.inc(ii, "qs", 16)
    for e in (t, v, s, g, sy):
        e.wait_ge(p.sem("qs"), p.cnt["qs"])
    p.barrier("phF3")
    scX.close()
    load_consts(p, ["psig", "negsel", "negi", "tri_q0", "tri_q2", "tri_t0", "tri_t2"])
    tabs = {}
    for k in ["cs", "snp", "ci", "sip"]:
        tabs[k] = p.sb("tb_" + k, [128, T], BF16)
        p.inc(g.dma_start(tabs[k][...], tb_h[k]), "qg", 16)
    cols = {}
    for k, h in [("qs", qs_h), ("ks", ks_h), ("qss", qss_h), ("kss", kss_h)]:
        cols[k] = p.sb("cl_" + k, [128, 6], F32)
        p.inc(sy.dma_start(cols[k][...], h), "qs", 16)
    lngb_s = p.sb("lngbs", [128, 2, 8, 6], F32)
    p.inc(sy.dma_start(lngb_s[...], lngb_h), "qs", 16)
    b1c_s = p.sb("b1cs", [128, 44, 6], F32)
    p.inc(sy.dma_start(b1c_s[...], b1c_h), "qs", 16)
    b2c_s = p.sb("b2cs", [128, 8, 6], F32)
    p.inc(sy.dma_start(b2c_s[...], b2c_h), "qs", 16)
    for li in range(6):
        emit_l1_ffn_layer(p, bkp, li, XLOC, wqkv_h[li], wo_h[li], w1_h[li], w2_h[li], tabs,
                          cols["qs"], cols["ks"], cols["qss"], cols["kss"],
                          lngb_s, b1c_s, b2c_s)
    p.barrier("fin")
    FA = p.sb("finA", [128, NCT, TCH], F32)
    FB = p.sb("finB", [128, NCT, TCH], F32)
    xloc_fm = XLOC.rearrange("(ct pp) t -> pp ct t", pp=128)
    x2s_fm = X2S.rearrange("(ct pp) t -> pp ct t", pp=128)
    for c in range(NCH):
        n0 = c * TCH
        if c > 0:
            sy.wait_ge(p.sem("v"), p.cnt["v"])
            sy.wait_ge(p.sem("qy"), p.cnt["qy"])
        sy.wait_ge(p.sem("qf"), p.cnt.get("qf", 0))
        il = sy.dma_start(FA[:, :, :], x2s_fm[:, :, n0:n0 + TCH])
        p.inc(il, "qs", 16)
        il = sy.dma_start(FB[:, :, :], xloc_fm[:, :, n0:n0 + TCH])
        tokf = p.inc(il, "qs", 16)
        v.wait_ge(p.sem("qs"), tokf)
        for ct in range(NCT):
            iv = v.tensor_add(FA[:, ct, :], FA[:, ct, :], FB[:, ct, :])
        vv = p.esw(iv, "v")
        g.wait_ge(p.sem("v"), vv)
        ig = g.dma_start(y_h[:, :, n0:n0 + TCH], FA[:, :, :])
        p.inc(ig, "qy", 16)
    g.wait_ge(p.sem("qy"), p.cnt["qy"])
    finish(p)
    return [r["y"] for r in run_program(p)]


FUSED = True


def kernel(**inputs):
    inp = inputs
    _host_prep.H = _host_prep(inp)
    x = inp["L0_attn_inputs"]
    xs = [np.ascontiguousarray(x[b].T) for b in range(8)]
    if FUSED:
        y = _launch_fused(inp, xs)
    else:
        x1 = _launch1(inp, xs)
        x1 = [np.ascontiguousarray(a) for a in x1]
        x2 = _launch2(inp, x1)
        y = _launch3(inp, [np.ascontiguousarray(a) for a in x2],
                     [np.ascontiguousarray(a) for a in x1])
    out = np.empty((8, T, D), np.float32)
    for b in range(8):
        out[b] = np.asarray(y[b]).transpose(1, 0, 2).reshape(D, T).T
    return out
